# version 01
# baseline (speedup 1.0000x reference)
import numpy as np
import ml_dtypes
from contextlib import ExitStack

import concourse.bass as bass
import concourse.mybir as mybir
from concourse.bass_utils import run_bass_kernel_spmd

F32 = mybir.dt.float32
BF16 = mybir.dt.bfloat16
AF = mybir.ActivationFunctionType
ALU = mybir.AluOpType
AX = mybir.AxisListType

D, C, S, T, H, NE = 1024, 8, 4096, 512, 16, 16
NT = S // T

W_J = [128, 256, 384, 512, 512, 384, 256, 128]
Q0_J = [0, 0, 0, 0, 0, 128, 256, 384]
DLT_J = [512, 384, 256, 128, 0, 0, 0, 0]


def _cols(widths):
    out, a = {}, 0
    for name, w in widths:
        out[name] = (a, a + w)
        a += w
    return out, a


CBF_COLS, CBF_W = _cols([
    ("HNEG", 1024), ("BIGM", 640), ("RFBD", 128), ("IDB", 128), ("ONE", 128),
    ("BR", 1024), ("CENT", 128), ("HSEL", 128), ("BVBF", 8), ("SELH", 2048),
])
C32_COLS, C32_W = _cols([
    ("ID32", 128), ("MUR", 8), ("MUK", 8), ("BOUT", 8), ("BQ", 8), ("BK", 8),
    ("BO", 8), ("BS0", 8), ("BS1", 8), ("EB", 1), ("BALL", 8), ("ONE32", 128), ("EPS5", 1), ("EPS30", 1), ("EBB", 16),
])


class Prog:
    serial = False

    def __init__(self, nc, ctx):
        self.nc = nc
        self.items = {k: [] for k in ("tensor", "vector", "scalar", "gpsimd", "sync")}
        self.sems, self.counts = {}, {}
        self.ctx = ctx

    def sem(self, name):
        self.sems[name] = self.ctx.enter_context(self.nc.semaphore(name))
        self.counts[name] = 0

    def at(self, name):
        return self.counts[name]

    def now(self, *names):
        return [(n, self.counts[n]) for n in names]

    def emit(self, engine, fn, waits=(), inc=None, by=1):
        tgt = None
        if self.serial:
            waits = list(waits) + [(k, v) for k, v in self.counts.items()]
        if inc is not None:
            self.counts[inc] += by
            tgt = self.counts[inc]
        waits = tuple((s, v) for (s, v) in waits if v > 0)

        def item(eng, fn=fn, waits=waits, inc=inc, by=by):
            for (s, v) in waits:
                eng.wait_ge(self.sems[s], v)
            ins = fn(eng)
            if inc is not None:
                ins.then_inc(self.sems[inc], by)

        self.items[engine].append(item)
        return tgt

    def replay_block(self):
        with self.nc.Block() as blk:
            for name in ("sync", "gpsimd", "tensor", "scalar", "vector"):
                fns = list(self.items[name])
                if not fns:
                    continue

                def runner(eng, fns=fns):
                    for f in fns:
                        f(eng)

                getattr(blk, name)(runner)
        self.items = {k: [] for k in self.items}


class Rotor:

    def __init__(self, n):
        self.free = [[] for _ in range(n)]
        self.n = n
        self.i = 0

    def acquire(self):
        s = self.i % self.n
        self.i += 1
        return s, list(self.free[s])

    def release(self, s, sem, count):
        self.free[s] = [(sem, count)]


def build_nc(stop_after=None, dbg=(), serial=False):
    nc = bass.Bass("TRN2", target_bir_lowering=False)
    din = {}

    def dram_in(name, shape):
        din[name] = nc.dram_tensor(name, list(shape), F32, kind="ExternalInput")
        return din[name]

    x_in = dram_in("x_fm", (D, S))
    for nm, sh in [("wqkvT", (D, 3 * D)), ("woutT", (D, D)), ("WrT", (D, D)),
                   ("WkT", (D, D)), ("WvT", (D, D)), ("Wq1T", (D, D)),
                   ("Wk1T", (D, D)), ("Wv1T", (D, D)), ("WoT", (D, D)),
                   ("WsT", (2, D, D)), ("WeT", (NE, D, D)),
                   ("ropein", (NT, 128, C * T)), ("cbf", (128, CBF_W)),
                   ("c32", (128, C32_W))]:
        dram_in(nm, sh)
    out_d = nc.dram_tensor("out_fm", [D, S], F32, kind="ExternalOutput")

    sc = {}
    for name, shape in [("wqkvT", [D, 3 * D]), ("woutT", [D, D]), ("WrT", [D, D]),
                        ("WkT", [D, D]), ("WvT", [D, D]), ("Wq1T", [D, D]),
                        ("Wk1T", [D, D]), ("Wv1T", [D, D]), ("WoT", [D, D]),
                        ("WsT", [2, D, D]), ("WeT", [NE, D, D]),
                        ("rope", [NT, 128, C * T])]:
        sc[name] = nc.dram_tensor("s_" + name, shape, BF16)
    s_g = nc.dram_tensor("s_g", [16, T], BF16)

    dbg_d = {}
    with ExitStack() as ctx:
        E = ctx.enter_context
        x_sb = E(nc.sbuf_tensor("x_sb", [128, C, S], BF16))
        xn_sb = E(nc.sbuf_tensor("xn_sb", [128, C, T + 1], BF16))
        WS = [E(nc.sbuf_tensor(f"WS{i}", [128, C, D], BF16)) for i in range(3)]
        ws2_off = nc.lookup_mloc(WS[2]).addr
        k_bufs = [nc.alloc_sbuf_tensor_at("kb0", [128, C, T], BF16, offset=ws2_off),
                  nc.alloc_sbuf_tensor_at("kb1", [128, C, T], BF16,
                                          offset=ws2_off + C * T * 2)]
        v_bufs = [E(nc.sbuf_tensor(f"vtm{i}", [128, 4, H, 65], BF16)) for i in range(2)]
        AR1 = E(nc.sbuf_tensor("AR1", [128, 4096], BF16))
        AR2 = E(nc.sbuf_tensor("AR2", [128, 4096], BF16))
        AR3 = E(nc.sbuf_tensor("AR3", [128, 4096], BF16))
        AR4 = E(nc.sbuf_tensor("AR4", [128, 4096], BF16))
        sqbuf = E(nc.sbuf_tensor("sqbuf", [128, 1024], BF16))
        rope_sb = E(nc.sbuf_tensor("rope_sb", [128, C, T], BF16))
        cbf = E(nc.sbuf_tensor("cbf_sb", [128, CBF_W], BF16))
        c32 = E(nc.sbuf_tensor("c32_sb", [128, C32_W], F32))
        stat32 = E(nc.sbuf_tensor("stat32", [128, 512], F32))
        misc32 = E(nc.sbuf_tensor("misc32", [128, 512], F32))
        gt32 = E(nc.sbuf_tensor("gt32", [128, 80], F32))
        mk2 = E(nc.sbuf_tensor("mk2", [16, 16], F32))
        ksum32 = E(nc.sbuf_tensor("ksum32", [128, 8], F32))
        kv_sb = E(nc.sbuf_tensor("kv_sb", [128, C, 128], BF16))
        ksd = E(nc.sbuf_tensor("ksd", [128, 128], BF16))
        mhat = E(nc.sbuf_tensor("mhat", [16, 512], BF16))
        g_sb = E(nc.sbuf_tensor("g_sb", [16, 512], BF16))
        xt32 = [E(nc.sbuf_tensor(f"xt32_{i}", [128, 512], F32)) for i in range(2)]
        ost = [E(nc.sbuf_tensor(f"ost{i}", [128, 512], F32)) for i in range(2)]
        gdump = E(nc.sbuf_tensor("gdump", [128, 64], F32)) if "gdump" in dbg else None
        P = [E(nc.psum_tensor(f"P{i}", [128, 512], F32)) for i in range(8)]

        def CBo(name):
            return CBF_COLS[name][0]

        def CB(name, rows=slice(0, 128), cols=None):
            a, b = CBF_COLS[name]
            if cols is not None:
                b = a + cols.stop
                a = a + cols.start
            return cbf[rows, a:b]

        def C3(name, rows=slice(0, 128), cols=None):
            a, b = C32_COLS[name]
            if cols is not None:
                b = a + cols.stop
                a = a + cols.start
            return c32[rows, a:b]

        ONEc = CB("ONE", cols=slice(0, 1))
        ONEr = C3("ONE32", rows=slice(0, 1))
        RFBD = CB("RFBD")
        IDB = CB("IDB")

        prog = Prog(nc, ctx)
        prog.serial = serial
        for sname in ("dG", "dW", "dR", "dO", "tS", "vS", "sS", "gS"):
            prog.sem(sname)

        def xc(c, t):
            return x_sb[:, c, T * t:T * (t + 1)]

        def xnc(c):
            return xn_sb[:, c, 1:T + 1]

        def xnp(c):
            return xn_sb[:, c, 0:T]

        def arc(buf, c):
            return buf[:, T * c:T * (c + 1)]

        def artm(buf, nb):
            return buf[:, 1024 * nb:1024 * (nb + 1)]

        SQ = [sqbuf[:, 512 * i:512 * (i + 1)] for i in range(2)]

        def d_gp(fn):
            return prog.emit("gpsimd", fn, inc="dG", by=16)

        d_gp(lambda g: g.dma_start(cbf[:, :], din["cbf"][:, :]))
        d_gp(lambda g: g.dma_start(c32[:, :], din["c32"][:, :]))
        d_gp(lambda g: g.dma_start(
            x_sb[:, :, :], bass.AP(din["x_fm"], 0, [[S, 128], [128 * S, C], [1, S]])))
        for nm in ("wqkvT", "woutT", "WrT", "WkT", "WvT"):
            d_gp(lambda g, nm=nm: g.dma_start(sc[nm][:, :], din[nm][:, :]))
        su = prog.at("dG")
        prog.emit("vector", lambda v: v.tensor_scalar_mul(RFBD, RFBD, 0.125),
                  waits=[("dG", su)], inc="vS")
        prog.emit("vector", lambda v: v.tensor_add(stat32[:, 0:8], C3("BS0"), C3("BS1")))
        prog.emit("vector", lambda v: v.tensor_scalar(
            stat32[:, 0:8], stat32[:, 0:8], 0.5, None, ALU.mult))
        prog.emit("vector", lambda v: v.tensor_add(C3("BALL"), stat32[:, 0:8], C3("BO")))
        prog.emit("vector", lambda v: v.memset(mk2[:, :], -1e30))
        for b in range(2):
            prog.emit("vector", lambda v, b=b: v.memset(v_bufs[b][:, :, :, 64:65], 1.0),
                      inc="vS")
        prog.replay_block()

        conv = {}
        for nm in ("Wq1T", "Wk1T", "Wv1T", "WoT"):
            conv[nm] = d_gp(lambda g, nm=nm: g.dma_start(sc[nm][:, :], din[nm][:, :]))
        for e2 in range(2):
            conv[f"ws{e2}"] = d_gp(
                lambda g, e2=e2: g.dma_start(sc["WsT"][e2, :, :], din["WsT"][e2, :, :]))
        for e in range(NE):
            conv[f"e{e}"] = d_gp(
                lambda g, e=e: g.dma_start(sc["WeT"][e, :, :], din["WeT"][e, :, :]))
        for t in range(NT):
            conv[f"rope{t}"] = d_gp(
                lambda g, t=t: g.dma_start(sc["rope"][t, :, :], din["ropein"][t, :, :]))

        def load_w(slot, src, col0, waits=()):
            ncols = src.shape[-1]

            def fn(sy, slot=slot, src=src, col0=col0, ncols=ncols):
                return sy.dma_start(
                    slot[:, :, :],
                    bass.AP(src, col0, [[ncols, 128], [128 * ncols, C], [1, D]]))

            return prog.emit("sync", fn, waits=waits, inc="dW", by=16)

        def load_w3(slot, src3, idx, waits=()):
            def fn(sy, slot=slot, src3=src3, idx=idx):
                return sy.dma_start(
                    slot[:, :, :],
                    bass.AP(src3, idx * D * D, [[D, 128], [128 * D, C], [1, D]]))

            return prog.emit("sync", fn, waits=waits, inc="dW", by=16)

        r_sq = Rotor(2)
        r_xt = Rotor(2)
        r_ost = Rotor(2)

        def emit_ln(t, p01_w, p23_w, xn_w=()):
            nm_ = stat32[0:1, 0:512]
            istd = misc32[0:1, 0:512]
            tmpA = stat32[32:33, 0:512]
            tk = 0
            for c in range(C):
                si, w_sq = r_sq.acquire()
                prog.emit("scalar", lambda s, c=c, si=si, t=t:
                          s.activation(SQ[si], xc(c, t), AF.Square),
                          waits=w_sq, inc="sS")
                s_now = prog.now("sS")
                prog.emit("tensor", lambda e, c=c, t=t:
                          e.matmul(P[0][0:1, :], ONEc, xc(c, t),
                                   start=(c == 0), stop=(c == C - 1)),
                          waits=(list(p01_w) if c == 0 else []))
                tk = prog.emit("tensor", lambda e, c=c, si=si:
                               e.matmul(P[1][0:1, :], ONEc, SQ[si],
                                        start=(c == 0), stop=(c == C - 1)),
                               waits=s_now, inc="tS")
                r_sq.release(si, "tS", tk)
            prog.emit("vector", lambda v:
                      v.tensor_scalar(nm_, P[0][0:1, :], -1.0 / D, None, ALU.mult),
                      waits=[("tS", tk)])
            prog.emit("vector", lambda v: v.tensor_mul(tmpA, nm_, nm_))
            prog.emit("vector", lambda v:
                      v.scalar_tensor_tensor(tmpA, P[1][0:1, :], 1.0 / D, tmpA,
                                             ALU.mult, ALU.subtract), inc="vS")
            vv = prog.now("vS")
            prog.emit("scalar", lambda s: s.activation(tmpA, tmpA, AF.Ln, bias=C3("EPS5", rows=slice(32, 33))),
                      waits=vv)
            prog.emit("scalar", lambda s: s.activation(istd, tmpA, AF.Exp, scale=-0.5),
                      inc="sS")
            sis = prog.now("sS")
            prog.emit("tensor", lambda e:
                      e.matmul(P[2][:, :], ONEr, nm_, start=True, stop=True),
                      waits=sis + list(p23_w))
            tb = prog.emit("tensor", lambda e:
                           e.matmul(P[3][:, :], ONEr, istd, start=True, stop=True),
                           inc="tS")
            for c in range(C):
                si, w_xt = r_xt.acquire()
                prog.emit("vector", lambda v, c=c, si=si, t=t:
                          v.tensor_add(xt32[si][:, :], xc(c, t), P[2][:, :]),
                          waits=[("tS", tb)] + w_xt + (list(xn_w) if c == 0 else []))
                vk = prog.emit("vector", lambda v, c=c, si=si:
                               v.tensor_mul(xnc(c), xt32[si][:, :], P[3][:, :]), inc="vS")
                r_xt.release(si, "vS", vk)
            return prog.at("vS")

        wk_ld = load_w(WS[0], sc["wqkvT"], D, waits=[("dG", su)])
        wv_ld = load_w(WS[1], sc["wqkvT"], 2 * D, waits=[("dG", su)])
        wq_ld = load_w(WS[2], sc["wqkvT"], 0, waits=[("dG", su)])

        KFM, KTM, VTM = AR1, AR2, AR4
        p01_w, p23_w = [], []
        for t in range(NT):
            vxn = emit_ln(t, p01_w, p23_w)
            ev = [[], []]
            for o in range(C):
                pb = P[o % 2]
                w0 = [("vS", vxn)] + ev[o % 2] + ([("dW", wk_ld)] if t == 0 and o == 0 else [])
                tk = 0
                for i in range(C):
                    tk = prog.emit("tensor", lambda e, o=o, i=i, pb=pb:
                                   e.matmul(pb[:, :], WS[0][:, i, 128 * o:128 * (o + 1)],
                                            xnc(i), start=(i == 0), stop=(i == C - 1)),
                                   waits=(w0 if i == 0 else []),
                                   inc=("tS" if i == C - 1 else None))
                se = prog.emit("scalar", lambda s, o=o, pb=pb:
                               s.activation(arc(KFM, o), pb[:, :], AF.Copy),
                               waits=[("tS", tk)], inc="sS")
                ev[o % 2] = [("sS", se)]
            for c in range(C):
                si, w_sq = r_sq.acquire()
                prog.emit("scalar", lambda s, c=c, si=si:
                          s.activation(SQ[si], arc(KFM, c), AF.Square), waits=w_sq,
                          inc="sS")
                s_now = prog.now("sS")
                tk = prog.emit("tensor", lambda e, c=c, si=si:
                               e.matmul(P[4][0:16, :],
                                        CB("HSEL", cols=slice(16 * c, 16 * c + 16)),
                                        SQ[si], start=(c == 0), stop=(c == C - 1)),
                               waits=s_now, inc="tS")
                r_sq.release(si, "tS", tk)
            prog.emit("scalar", lambda s:
                      s.activation(misc32[0:16, :], P[4][0:16, :], AF.Ln, bias=C3("EPS30", rows=slice(0, 16))),
                      waits=[("tS", tk)])
            skn = prog.emit("scalar", lambda s:
                            s.activation(mhat[:, :], misc32[0:16, :], AF.Exp, scale=0.5),
                            inc="sS")
            vkn = prog.emit("vector", lambda v: v.engine_nop(),
                            waits=[("sS", skn)], inc="vS")
            ktm_ev, vtm_ev = [], []
            for nb in range(4):
                w0 = [("vS", vkn)] + ktm_ev
                for c in range(C):
                    pb = P[2 + c // 4]
                    cols = slice(128 * (c % 4), 128 * (c % 4) + 128)
                    prog.emit("tensor", lambda e, nb=nb, c=c, pb=pb, cols=cols:
                              e.matmul(pb[:, cols],
                                       KFM[:, T * c + 128 * nb:T * c + 128 * (nb + 1)],
                                       RFBD, start=True, stop=False),
                              waits=(w0 if c == 0 else []))
                    prog.emit("tensor", lambda e, nb=nb, c=c, pb=pb, cols=cols:
                              e.matmul(pb[:, cols], mhat[0:16, 128 * nb:128 * (nb + 1)],
                                       CB("HNEG", rows=slice(0, 16),
                                          cols=slice(128 * c, 128 * c + 128)),
                                       start=False, stop=True),
                              inc=("tS" if c in (3, 7) else None))
                tpt = prog.at("tS")
                ktm_ev = []
                for hf in range(2):
                    se = prog.emit("scalar", lambda s, nb=nb, hf=hf:
                                   s.activation(artm(KTM, nb)[:, 512 * hf:512 * (hf + 1)],
                                                P[2 + hf][:, :], AF.Exp),
                                   waits=[("tS", tpt)], inc="sS")
                    ktm_ev = [("sS", se)]
                w1 = list(vtm_ev) + ([("dW", wv_ld)] if (t == 0 and nb == 0) else [])
                if nb == 0:
                    w1 += ev[0] + ev[1]
                for i in range(C):
                    for hf in range(2):
                        prog.emit("tensor", lambda e, nb=nb, i=i, hf=hf:
                                  e.matmul(P[hf][:, :],
                                           xn_sb[:, i, 1 + 128 * nb:1 + 128 * (nb + 1)],
                                           WS[1][:, i, 512 * hf:512 * (hf + 1)],
                                           start=(i == 0), stop=(i == C - 1)),
                                  waits=(w1 if (i == 0 and hf == 0) else []),
                                  inc=("tS" if (i == C - 1 and hf == 1) else None))
                tv = prog.at("tS")
                vtm_ev = []
                for hf in range(2):
                    se = prog.emit("scalar", lambda s, nb=nb, hf=hf:
                                   s.activation(artm(VTM, nb)[:, 512 * hf:512 * (hf + 1)],
                                                P[hf][:, :], AF.Copy),
                                   waits=[("tS", tv)], inc="sS")
                    vtm_ev = [("sS", se)]
                sv = prog.now("sS")
                first = (t == 0 and nb == 0)
                for c in range(C):
                    pb = P[5 + c // 4]
                    cols = slice(128 * (c % 4), 128 * (c % 4) + 128)
                    prog.emit("tensor",
                              lambda e, nb=nb, c=c, pb=pb, cols=cols, first=first:
                              e.matmul(pb[:, cols],
                                       artm(KTM, nb)[:, 128 * c:128 * (c + 1)],
                                       artm(VTM, nb)[:, 128 * c:128 * (c + 1)],
                                       start=(first and (c % 4) == 0), stop=False),
                              waits=(sv if c == 0 else []))
                    prog.emit("tensor", lambda e, nb=nb, c=c, first=first:
                              e.matmul(P[7][:, c:c + 1],
                                       artm(KTM, nb)[:, 128 * c:128 * (c + 1)],
                                       ONEc, start=(first and c == 0), stop=False),
                              inc=("tS" if c == C - 1 else None))
            pv = prog.emit("vector", lambda v: v.engine_nop(),
                           waits=prog.now("sS"), inc="vS")
            p01_w = [("vS", pv)]
            p23_w = [("vS", pv)]

        tall = prog.at("tS")
        prog.emit("vector", lambda v: v.memset(kv_sb[:, :, :], 0.0), waits=[("tS", tall)])
        for c in range(C):
            pb = P[5 + c // 4]
            c0 = 128 * (c % 4)
            for hh in range(2):
                sl = slice(64 * hh, 64 * hh + 64)
                prog.emit("scalar", lambda s, c=c, pb=pb, c0=c0, sl=sl, hh=hh:
                          s.activation(kv_sb[sl, c, 64 * hh:64 * hh + 64],
                                       pb[sl, c0 + 64 * hh:c0 + 64 * hh + 64], AF.Copy),
                          waits=[("tS", tall)], inc="sS")
        prog.emit("vector", lambda v: v.tensor_copy(ksum32[:, :], P[7][:, 0:8]),
                  waits=[("tS", tall)])
        prog.emit("vector", lambda v: v.memset(ksd[:, :], 0.0))
        for c in range(C):
            for hh in range(2):
                sl = slice(64 * hh, 64 * hh + 64)
                col = 16 * c + 2 * c + hh
                prog.emit("vector", lambda v, c=c, sl=sl, col=col:
                          v.tensor_copy(ksd[sl, col:col + 1], ksum32[sl, c:c + 1]),
                          inc="vS")
        mid_v = prog.at("vS")
        mid_s = prog.at("sS")

        if stop_after == "l0p1":
            _finish(nc, prog, dbg, dbg_d, locals())
            return nc, din, out_d, dbg_d

        QP, ATT, XKB, XRB = AR1, AR2, AR3, AR4
        p01_w = [("vS", mid_v)]
        p23_w = [("vS", mid_v)]
        ws_free = [[("sS", mid_s)], [("sS", mid_s)]]
        for t in range(NT):
            wWr = load_w(WS[0], sc["WrT"], 0, waits=ws_free[0])
            wWk = load_w(WS[1], sc["WkT"], 0, waits=ws_free[1])
            gw = prog.now("gS")
            if t == 0:
                prog.emit("vector", lambda v: v.memset(xn_sb[:, :, 0:1], 0.0),
                          waits=[("vS", mid_v), ("sS", mid_s)], inc="vS")
            else:
                for c in range(C):
                    prog.emit("vector", lambda v, c=c:
                              v.tensor_copy(xn_sb[:, c, 0:1], xn_sb[:, c, T:T + 1]),
                              waits=(gw if c == 0 else []), inc="vS")
            vxn = emit_ln(t, p01_w, p23_w, xn_w=gw)
            ev = [[], []]
            tk = 0
            for o in range(C):
                pb = P[o % 2]
                w0 = [("vS", vxn)] + ev[o % 2] + ([("dW", wq_ld)] if t == 0 and o == 0 else [])
                for i in range(C):
                    tk = prog.emit("tensor", lambda e, o=o, i=i, pb=pb:
                                   e.matmul(pb[:, :], WS[2][:, i, 128 * o:128 * (o + 1)],
                                            xnc(i), start=(i == 0), stop=(i == C - 1)),
                                   waits=(w0 if i == 0 else []),
                                   inc=("tS" if i == C - 1 else None))
                se = prog.emit("scalar", lambda s, o=o, pb=pb:
                               s.activation(arc(XRB, o), pb[:, :], AF.Copy),
                               waits=[("tS", tk)], inc="sS")
                ev[o % 2] = [("sS", se)]
                si, w_sq = r_sq.acquire()
                prog.emit("scalar", lambda s, o=o, si=si:
                          s.activation(SQ[si], arc(XRB, o), AF.Square), waits=w_sq,
                          inc="sS")
                s_now = prog.now("sS")
                tq = prog.emit("tensor", lambda e, o=o, si=si:
                               e.matmul(P[4][0:16, :],
                                        CB("HSEL", cols=slice(16 * o, 16 * o + 16)),
                                        SQ[si], start=(o == 0), stop=(o == C - 1)),
                               waits=s_now, inc="tS")
                r_sq.release(si, "tS", tq)
            prog.emit("scalar", lambda s:
                      s.activation(misc32[0:16, :], P[4][0:16, :], AF.Ln, bias=C3("EPS30", rows=slice(0, 16))),
                      waits=[("tS", tq)])
            sqn = prog.emit("scalar", lambda s:
                            s.activation(mhat[:, :], misc32[0:16, :], AF.Exp, scale=0.5),
                            inc="sS")
            vqn = prog.emit("vector", lambda v: v.engine_nop(),
                            waits=[("sS", sqn)], inc="vS")
            qev = [[], []]
            tfeat = 0
            for c in range(C):
                pb = P[2 + c % 2]
                w0 = [("vS", vqn)] + qev[c % 2]
                prog.emit("tensor", lambda e, c=c, pb=pb:
                          e.matmul(pb[:, :], RFBD, arc(XRB, c), start=True, stop=False),
                          waits=w0)
                tfeat = prog.emit("tensor", lambda e, c=c, pb=pb:
                                  e.matmul(pb[:, :],
                                           CB("HNEG", rows=slice(0, 16),
                                              cols=slice(128 * c, 128 * c + 128)),
                                           mhat[:, :], start=False, stop=True), inc="tS")
                se = prog.emit("scalar", lambda s, c=c, pb=pb:
                               s.activation(arc(QP, c), pb[:, :], AF.Exp),
                               waits=[("tS", tfeat)], inc="sS")
                qev[c % 2] = [("sS", se)]
            sqp = prog.at("sS")
            aev = [[("sS", sqp)], [("sS", sqp)]]
            tden = 0
            for c in range(C):
                pb = P[c % 2]
                prog.emit("tensor", lambda e, c=c, pb=pb:
                          e.matmul(pb[:, :], kv_sb[:, c, :], arc(QP, c),
                                   start=True, stop=True), waits=aev[c % 2])
                tden = prog.emit("tensor", lambda e, c=c:
                                 e.matmul(P[4][0:16, :], ksd[:, 16 * c:16 * (c + 1)],
                                          arc(QP, c), start=(c == 0), stop=(c == C - 1)),
                                 inc="tS")
                se = prog.emit("scalar", lambda s, c=c, pb=pb:
                               s.activation(arc(ATT, c), pb[:, :], AF.Copy),
                               waits=[("tS", tden)], inc="sS")
                aev[c % 2] = [("sS", se)]
            prog.emit("vector", lambda v:
                      v.tensor_scalar(misc32[0:16, :], P[4][0:16, :], -1.0, -1e-6,
                                      ALU.mult, ALU.add), waits=[("tS", tden)])
            prog.emit("vector", lambda v:
                      v.reciprocal(misc32[0:16, :], misc32[0:16, :]))
            vrd = prog.emit("vector", lambda v: v.tensor_copy(mhat[:, :], misc32[0:16, :]),
                            inc="vS")
            amul = [[("vS", vrd)], [("vS", vrd)]]
            for c in range(C):
                pb = P[5 + c % 2]
                tb2 = prog.emit("tensor", lambda e, c=c, pb=pb:
                                e.matmul(pb[:, :],
                                         CB("HNEG", rows=slice(0, 16),
                                            cols=slice(128 * c, 128 * c + 128)),
                                         mhat[:, :], start=True, stop=True),
                                waits=amul[c % 2] + aev[c % 2], inc="tS")
                av = prog.emit("vector", lambda v, c=c, pb=pb:
                               v.tensor_mul(arc(ATT, c), arc(ATT, c), pb[:, :]),
                               waits=[("tS", tb2)], inc="vS")
                amul[c % 2] = [("vS", av)]
            vatt = prog.at("vS")
            gP = 0
            for c in range(C):
                w0 = [("sS", sqp), ("tS", tden)] if c == 0 else []
                prog.emit("gpsimd", lambda g, c=c:
                          g.tensor_sub(arc(XRB, c), xnc(c), xnp(c)), waits=w0)
                gsub = prog.now("gS")
                prog.emit("gpsimd", lambda g, c=c: g.nop(), inc="gS")
                prog.emit("vector", lambda v, c=c:
                          v.scalar_tensor_tensor(arc(XKB, c), arc(XRB, c),
                                                 C3("MUK", cols=slice(c, c + 1)),
                                                 xnp(c), ALU.mult, ALU.add),
                          waits=[("gS", prog.at("gS"))])
                prog.emit("vector", lambda v, c=c:
                          v.scalar_tensor_tensor(arc(XRB, c), arc(XRB, c),
                                                 C3("MUR", cols=slice(c, c + 1)),
                                                 xnp(c), ALU.mult, ALU.add),
                          inc="vS")
            gP = prog.at("vS")
            rv = [[], []]
            for o in range(C):
                pb = P[2 + o % 2]
                w0 = ([("vS", gP), ("dW", wWr)] if o == 0 else []) + rv[o % 2]
                for i in range(C):
                    tk = prog.emit("tensor", lambda e, o=o, i=i, pb=pb:
                                   e.matmul(pb[:, :], WS[0][:, i, 128 * o:128 * (o + 1)],
                                            arc(XRB, i), start=(i == 0), stop=(i == C - 1)),
                                   waits=(w0 if i == 0 else []),
                                   inc=("tS" if i == C - 1 else None))
                si, w_xt = r_xt.acquire()
                prog.emit("scalar", lambda s, pb=pb, si=si:
                          s.activation(xt32[si][:, :], pb[:, :], AF.Exp, scale=-1.0),
                          waits=[("tS", tk)] + w_xt, inc="sS")
                s_now = prog.now("sS")
                prog.emit("vector", lambda v, si=si:
                          v.tensor_scalar(xt32[si][:, :], xt32[si][:, :], 1.0, None, ALU.add),
                          waits=s_now)
                def _recip(v, o=o, si=si):
                    with nc.allow_low_precision(reason="sigmoid gate stored bf16"):
                        return v.reciprocal(arc(QP, o), xt32[si][:, :])
                rr = prog.emit("vector", _recip, inc="vS")
                rv[o % 2] = [("vS", rr)]
                r_xt.release(si, "vS", rr)
            kv2 = [[], []]
            for o in range(C):
                pb = P[2 + o % 2]
                w0 = ([("dW", wWk)] + prog.now("vS") if o == 0 else []) + kv2[o % 2]
                for i in range(C):
                    tk = prog.emit("tensor", lambda e, o=o, i=i, pb=pb:
                                   e.matmul(pb[:, :], WS[1][:, i, 128 * o:128 * (o + 1)],
                                            arc(XKB, i), start=(i == 0), stop=(i == C - 1)),
                                   waits=(w0 if i == 0 else []),
                                   inc=("tS" if i == C - 1 else None))
                prog.emit("scalar", lambda s, o=o, pb=pb:
                          s.activation(arc(XRB, o), pb[:, :], AF.Relu),
                          waits=[("tS", tk)], inc="sS")
                se = prog.emit("scalar", lambda s, o=o:
                               s.activation(arc(XRB, o), arc(XRB, o), AF.Square),
                               inc="sS")
                kv2[o % 2] = [("sS", se)]
            skact = prog.at("sS")
            wWv = load_w(WS[0], sc["WvT"], 0, waits=prog.now("tS"))
            wWo = load_w(WS[1], sc["woutT"], 0, waits=prog.now("tS"))
            tmul = [[], []]
            for o in range(C):
                pb = P[o % 2]
                w0 = ([("dW", wWv), ("sS", skact)] if o == 0 else []) + tmul[o % 2]
                for i in range(C):
                    tk = prog.emit("tensor", lambda e, o=o, i=i, pb=pb:
                                   e.matmul(pb[:, :], WS[0][:, i, 128 * o:128 * (o + 1)],
                                            arc(XRB, i), start=(i == 0), stop=(i == C - 1)),
                                   waits=(w0 if i == 0 else []),
                                   inc=("tS" if i == C - 1 else None))
                tm = prog.emit("vector", lambda v, o=o, pb=pb:
                               v.tensor_mul(arc(QP, o), arc(QP, o), pb[:, :]),
                               waits=[("tS", tk)], inc="vS")
                tmul[o % 2] = [("vS", tm)]
            xuv = [[], []]
            for o in range(C):
                pb = P[2 + o % 2]
                w0 = ([("dW", wWo), ("vS", vatt)] if o == 0 else []) + xuv[o % 2]
                for i in range(C):
                    tk = prog.emit("tensor", lambda e, o=o, i=i, pb=pb:
                                   e.matmul(pb[:, :], WS[1][:, i, 128 * o:128 * (o + 1)],
                                            arc(ATT, i), start=(i == 0), stop=(i == C - 1)),
                                   waits=(w0 if i == 0 else []),
                                   inc=("tS" if i == C - 1 else None))
                si, w_xt = r_xt.acquire()
                prog.emit("vector", lambda v, o=o, pb=pb, si=si, t=t:
                          v.scalar_tensor_tensor(xt32[si][:, :], pb[:, :],
                                                 C3("BOUT", cols=slice(o, o + 1)),
                                                 xc(o, t), ALU.add, ALU.add),
                          waits=[("tS", tk)] + tmul[o % 2] + w_xt)
                xv = prog.emit("vector", lambda v, o=o, si=si, t=t:
                               v.tensor_add(xc(o, t), xt32[si][:, :], arc(QP, o)), inc="vS")
                xuv[o % 2] = [("vS", xv)]
                r_xt.release(si, "vS", xv)
            ws_free = [prog.now("tS"), prog.now("tS")]
            p01_w = prog.now("vS")
            p23_w = prog.now("vS")

        if stop_after == "l0":
            _finish(nc, prog, dbg, dbg_d, locals())
            return nc, din, out_d, dbg_d

        QROT, ATT1, XH = AR1, AR2, AR4
        ZU = [AR3[:, 512 * i:512 * (i + 1)] for i in range(2)]
        PRB = [AR3[:, 1024 + 512 * i:1024 + 512 * (i + 1)] for i in range(2)]
        GB = [AR3[:, 2048 + 512 * i:2048 + 512 * (i + 1)] for i in range(2)]
        r_zu = Rotor(2)
        r_prb = Rotor(2)
        gb_free = [[], []]
        mg_prev = []
        rope_used = []
        for t in range(NT):
            cur, prv = t % 2, 1 - (t % 2)
            names = ["Wq1T", "Wk1T", "Wv1T", "WoT", "ws0", "ws1"] + \
                    [f"e{e}" for e in range(NE)]
            ld_tick, slot_of = {}, {}
            ws_user_done = {0: list(ws_free[0]), 1: list(ws_free[1])}
            next_load = [0]

            def issue_load():
                idx = next_load[0]
                next_load[0] += 1
                nm = names[idx]
                sl = idx % 2
                slot_of[nm] = sl
                w = list(ws_user_done[sl])
                if t == 0:
                    w.append(("dG", conv.get(nm, conv.get(nm.replace("T", ""), 0))
                              if nm in conv else conv[nm]))
                if nm == "ws0":
                    ld_tick[nm] = load_w3(WS[sl], sc["WsT"], 0, waits=w)
                elif nm == "ws1":
                    ld_tick[nm] = load_w3(WS[sl], sc["WsT"], 1, waits=w)
                elif nm.startswith("e"):
                    ld_tick[nm] = load_w3(WS[sl], sc["WeT"], int(nm[1:]), waits=w)
                else:
                    ld_tick[nm] = load_w(WS[sl], sc[nm], 0, waits=w)

            issue_load()
            issue_load()

            def group_done(nm):
                ws_user_done[slot_of[nm]] = prog.now("tS")
                if next_load[0] < len(names):
                    issue_load()

            rp = prog.emit("sync", lambda sy, t=t:
                           sy.dma_start(rope_sb[:, :, :], sc["rope"][t, :, :]),
                           waits=list(rope_used) + [("dG", conv[f"rope{t}"])],
                           inc="dW", by=16)

            gw = prog.now("gS")
            if t == 0:
                prog.emit("vector", lambda v: v.memset(xn_sb[:, :, 0:1], 0.0),
                          waits=list(mg_prev), inc="vS")
            vxn = emit_ln(t, p01_w, p23_w, xn_w=gw)
            for c in range(C):
                prog.emit("vector", lambda v, c=c:
                          v.tensor_scalar(arc(XH, c), xnc(c), 0.5, None, ALU.mult),
                          inc=("vS" if c == C - 1 else None))
            vxh = prog.at("vS")
            for i in range(C):
                prog.emit("tensor", lambda e, i=i:
                          e.matmul(P[4][0:16, :],
                                   CB("CENT", cols=slice(16 * i, 16 * i + 16)),
                                   xnc(i), start=(i == 0), stop=(i == C - 1)),
                          waits=([("vS", vxn)] if i == 0 else []),
                          inc=("tS" if i == C - 1 else None))
            tg = prog.at("tS")
            prog.emit("scalar", lambda s:
                      s.activation(misc32[0:16, :], P[4][0:16, :], AF.Copy),
                      waits=[("tS", tg)], inc="sS")
            sg = prog.now("sS")
            if gdump is not None:
                prog.emit("vector", lambda v: v.tensor_copy(gdump[0:16, 32:48],
                                                            misc32[0:16, 0:16]),
                          waits=sg, inc="vS")
            vtr = []
            for nb in range(4):
                ttr = prog.emit("tensor", lambda e, nb=nb:
                                e.transpose(P[5][:, 0:16],
                                            misc32[0:16, 128 * nb:128 * (nb + 1)],
                                            C3("ID32", rows=slice(0, 16),
                                               cols=slice(0, 16))),
                                waits=sg + vtr, inc="tS")
                vv = prog.emit("vector", lambda v, nb=nb:
                               v.tensor_copy(gt32[:, 16 * nb:16 * (nb + 1)],
                                             P[5][:, 0:16]),
                               waits=[("tS", ttr)], inc="vS")
                vtr = [("vS", vv)]
            if gdump is not None:
                prog.emit("vector", lambda v: v.tensor_copy(gdump[:, 0:16],
                                                            gt32[:, 0:16]),
                          waits=vtr, inc="vS")
            prog.emit("vector", lambda v: v.nop(),
                      waits=prog.now("tS", "sS", "gS", "dW", "dR", "dG", "dO"),
                      inc="vS")
            nm1 = misc32[:, 0:1]
            sm_ = misc32[:, 2:3]
            rs_ = misc32[:, 8:9]
            m2_ = misc32[:, 4:5]

            def rep2_in(coloff, w):
                return bass.AP(gt32, coloff, [[80, 128], [0, 2], [1, w]])

            def out2(c0):
                return misc32[:, c0:c0 + 2].rearrange("p (a b) -> p a b", b=1)

            tmp16 = gt32[:, 64:80]
            for nb in range(4):
                blk = gt32[:, 16 * nb:16 * nb + 16]
                prog.emit("vector", lambda v, blk=blk:
                          v.tensor_add(blk, blk, C3("EBB")),
                          waits=(vtr if nb == 0 else []))
                prog.emit("vector", lambda v, nb=nb:
                          v.reduce_max(out2(0), rep2_in(16 * nb, 16), AX.X))
                prog.emit("vector", lambda v:
                          v.tensor_scalar(misc32[:, 0:4], misc32[:, 0:4], -1.0,
                                          None, ALU.mult), inc="vS")
                vs0 = prog.now("vS")
                prog.emit("scalar", lambda s, blk=blk:
                          s.activation(blk, blk, AF.Exp, bias=nm1), waits=vs0, inc="sS")
                ss0 = prog.now("sS")
                prog.emit("vector", lambda v, nb=nb:
                          v.reduce_sum(out2(2), rep2_in(16 * nb, 16), AX.X), waits=ss0)
                prog.emit("vector", lambda v:
                          v.reciprocal(misc32[:, 8:12], misc32[:, 2:6]))
                prog.emit("vector", lambda v, blk=blk:
                          v.tensor_scalar(blk, blk, rs_, None, ALU.mult))
                prog.emit("vector", lambda v, nb=nb:
                          v.reduce_max(out2(4), rep2_in(16 * nb, 16), AX.X))
                prog.emit("vector", lambda v, blk=blk:
                          v.tensor_scalar(tmp16, blk, m2_, None, ALU.subtract))
                prog.emit("vector", lambda v:
                          v.tensor_scalar(tmp16, tmp16, 0.0, None, ALU.is_ge))
                prog.emit("vector", lambda v, blk=blk:
                          v.scalar_tensor_tensor(tmp16, tmp16, -1e30, blk,
                                                 ALU.mult, ALU.add))
                prog.emit("vector", lambda v, nb=nb:
                          v.reduce_max(out2(4), bass.AP(gt32, 64, [[80, 128], [0, 2], [1, 16]]),
                                       AX.X))
                prog.emit("vector", lambda v, blk=blk:
                          v.tensor_scalar(tmp16, blk, m2_, None, ALU.is_ge))
                prog.emit("vector", lambda v, blk=blk:
                          v.tensor_mul(blk, blk, tmp16), inc="vS")
            vg = prog.at("vS")
            str_ = []
            for nb in range(4):
                ttr = prog.emit("tensor", lambda e, nb=nb:
                                e.transpose(P[5][0:16, 0:128],
                                            gt32[:, 16 * nb:16 * nb + 16], C3("ID32")),
                                waits=[("vS", vg)] + str_, inc="tS")
                se = prog.emit("scalar", lambda s, nb=nb:
                               s.activation(g_sb[0:16, 128 * nb:128 * (nb + 1)],
                                            P[5][0:16, 0:128], AF.Copy),
                               waits=[("tS", ttr)], inc="sS")
                str_ = [("sS", se)]
            gdma = prog.emit("scalar", lambda s: s.dma_start(s_g[:, :], g_sb[:, :]),
                             waits=str_, inc="dR", by=16)

            def qk_rope(dstbuf, WTnm, Bname, vfree0):
                ev_pair = [[("vS", vfree0)], [("vS", vfree0)]]
                last = 0
                for cp in range(4):
                    plo, phi = P[2 * (cp % 2)], P[1 + 2 * (cp % 2)]
                    w0 = [("dW", ld_tick[WTnm])] + ev_pair[cp % 2]
                    for (pp, cc) in ((plo, cp), (phi, cp + 4)):
                        for i in range(C):
                            prog.emit("tensor", lambda e, cc=cc, i=i, pp=pp, WTnm=WTnm:
                                      e.matmul(pp[:, :],
                                               WS[slot_of[WTnm]][:, i,
                                                                 128 * cc:128 * (cc + 1)],
                                               xnc(i), start=(i == 0),
                                               stop=(i == C - 1)),
                                      waits=(w0 if (i == 0 and cc == cp) else []),
                                      inc=("tS" if i == C - 1 else None))
                    tqk = prog.at("tS")
                    cosv = rope_sb[:, cp, :]
                    sinv = rope_sb[:, 4 + cp, :]
                    blo = C3(Bname, cols=slice(cp, cp + 1))
                    bhi = C3(Bname, cols=slice(cp + 4, cp + 5))
                    wv = [("tS", tqk), ("dW", rp)]
                    prog.emit("vector", lambda v, plo=plo, blo=blo, cosv=cosv:
                              v.scalar_tensor_tensor(xt32[0][:, :], plo[:, :], blo, cosv,
                                                     ALU.add, ALU.mult), waits=wv)
                    prog.emit("vector", lambda v, phi=phi, bhi=bhi, sinv=sinv:
                              v.scalar_tensor_tensor(xt32[1][:, :], phi[:, :], bhi, sinv,
                                                     ALU.add, ALU.mult))
                    prog.emit("vector", lambda v, dstbuf=dstbuf, cp=cp:
                              v.tensor_sub(_ch(dstbuf, cp), xt32[0][:, :], xt32[1][:, :]))
                    prog.emit("vector", lambda v, plo=plo, blo=blo, sinv=sinv:
                              v.scalar_tensor_tensor(xt32[0][:, :], plo[:, :], blo, sinv,
                                                     ALU.add, ALU.mult))
                    prog.emit("vector", lambda v, phi=phi, bhi=bhi, cosv=cosv:
                              v.scalar_tensor_tensor(xt32[1][:, :], phi[:, :], bhi, cosv,
                                                     ALU.add, ALU.mult))
                    last = prog.emit("vector", lambda v, dstbuf=dstbuf, cp=cp:
                                     v.tensor_add(_ch(dstbuf, cp + 4), xt32[0][:, :], xt32[1][:, :]),
                                     inc="vS")
                    ev_pair[cp % 2] = [("vS", last)]
                return last

            vq = qk_rope(QROT, "Wq1T", "BQ", vxn)
            group_done("Wq1T")
            vk = qk_rope(k_bufs[cur], "Wk1T", "BK", vq)
            group_done("Wk1T")
            vev = []
            sv1 = 0
            for nb in range(4):
                w0 = ([("dW", ld_tick["Wv1T"])] + str_ + [("vS", vxn)]
                      if nb == 0 else []) + vev
                for i in range(C):
                    for hf in range(2):
                        prog.emit("tensor", lambda e, nb=nb, i=i, hf=hf:
                                  e.matmul(P[5 + hf][:, :],
                                           xn_sb[:, i, 1 + 128 * nb:1 + 128 * (nb + 1)],
                                           WS[slot_of["Wv1T"]][:, i,
                                                               512 * hf:512 * (hf + 1)],
                                           start=(i == 0), stop=(i == C - 1)),
                                  waits=(w0 if (i == 0 and hf == 0) else []),
                                  inc=("tS" if (i == C - 1 and hf == 1) else None))
                tvv = prog.at("tS")
                vev = []
                for hf in range(2):
                    sv1 = prog.emit("scalar", lambda s, nb=nb, hf=hf, cur=cur:
                                    s.activation(
                                        v_bufs[cur][:, nb, 8 * hf:8 * hf + 8, 0:64],
                                        P[5 + hf][:, :], AF.Copy),
                                    waits=[("tS", tvv)], inc="sS")
                    vev = [("sS", sv1)]
            group_done("Wv1T")
            tk2 = 0
            for c in range(C):
                si, w_sq = r_sq.acquire()
                prog.emit("scalar", lambda s, c=c, si=si, cur=cur:
                          s.activation(SQ[si], k_bufs[cur][:, c, :], AF.Square),
                          waits=w_sq + ([("vS", vk)] if c == 0 else []), inc="sS")
                s_now = prog.now("sS")
                tk2 = prog.emit("tensor", lambda e, c=c, si=si:
                                e.matmul(P[4][0:16, :],
                                         CB("HSEL", cols=slice(16 * c, 16 * c + 16)),
                                         SQ[si], start=(c == 0), stop=(c == C - 1)),
                                waits=s_now, inc="tS")
                r_sq.release(si, "tS", tk2)
            prog.emit("vector", lambda v, cur=cur:
                      v.reduce_max(mk2[0:16, 4 * cur:4 * cur + 4]
                                   .rearrange("p (a b) -> p a b", b=1),
                                   bass.AP(P[4], 0, [[512, 16], [0, 4], [1, 512]]),
                                   AX.X),
                      waits=[("tS", tk2)])
            vmk = prog.emit("vector", lambda v: v.nop(), inc="vS")
            tq2 = 0
            for c in range(C):
                si, w_sq = r_sq.acquire()
                prog.emit("scalar", lambda s, c=c, si=si:
                          s.activation(SQ[si], _ch(QROT, c), AF.Square),
                          waits=w_sq + ([("vS", vq)] if c == 0 else []), inc="sS")
                s_now = prog.now("sS")
                tq2 = prog.emit("tensor", lambda e, c=c, si=si:
                                e.matmul(P[4][0:16, :],
                                         CB("HSEL", cols=slice(16 * c, 16 * c + 16)),
                                         SQ[si], start=(c == 0), stop=(c == C - 1)),
                                waits=s_now + ([("vS", vmk)] if c == 0 else []),
                                inc="tS")
                r_sq.release(si, "tS", tq2)
            prog.emit("vector", lambda v:
                      v.tensor_scalar(misc32[0:16, :], P[4][0:16, :], mk2[0:16, 0:1],
                                      0.5, ALU.add, ALU.mult),
                      waits=[("tS", tq2)])
            prog.emit("vector", lambda v:
                      v.tensor_scalar(stat32[0:16, 0:512], P[4][0:16, :], mk2[0:16, 4:5],
                                      0.5, ALU.add, ALU.mult))
            vmh = prog.emit("vector", lambda v:
                            v.tensor_max(mhat[:, :], misc32[0:16, :], stat32[0:16, 0:512]),
                            waits=[], inc="vS")

            js = list(range(8)) if t > 0 else list(range(4, 8))
            sc_free = [[("vS", vmh)], [("vS", vmh)]]
            pv_banks, bc_banks = [2, 5], [3, 6]
            pv_free = [[("vS", vmh)], [("sS", sv1)]]
            bc_free = [[("vS", vmh)], [("sS", sv1)]]
            attn_v = 0
            for h in range(H):
                pvb, bcb = P[pv_banks[h % 2]], P[bc_banks[h % 2]]
                r0 = 64 * (h % 2)
                ch = h // 2
                for jn, j in enumerate(js):
                    jj = j % 4
                    kb = k_bufs[prv] if j < 4 else k_bufs[cur]
                    vb = v_bufs[prv] if j < 4 else v_bufs[cur]
                    w_, q0, dl = W_J[j], Q0_J[j], DLT_J[j]
                    psb = P[jn % 2]
                    prog.emit("tensor", lambda e, kb=kb, jj=jj, r0=r0, ch=ch,
                              psb=psb, q0=q0, w_=w_:
                              e.matmul(psb[0:128, 0:w_],
                                       kb[r0:r0 + 64, ch, 128 * jj:128 * (jj + 1)],
                                       QROT[r0:r0 + 64, T * ch + q0:T * ch + q0 + w_],
                                       start=True, stop=False),
                              waits=sc_free[jn % 2])
                    tsc = prog.emit("tensor", lambda e, h=h, psb=psb, q0=q0, w_=w_:
                                    e.matmul(psb[0:128, 0:w_],
                                             CB("SELH", rows=slice(0, 16),
                                                cols=slice(128 * h, 128 * h + 128)),
                                             mhat[0:16, q0:q0 + w_],
                                             start=False, stop=True), inc="tS")
                    si, w_pr = r_prb.acquire()
                    sexp = prog.emit("scalar", lambda s, si=si, psb=psb, w_=w_:
                                     s.activation(PRB[si][:, 0:w_], psb[0:128, 0:w_],
                                                  AF.Exp, scale=0.125),
                                     waits=[("tS", tsc)] + w_pr, inc="sS")
                    sc_free[jn % 2] = [("sS", sexp)]
                    eng = "vector" if (j % 2 == 0) else "gpsimd"
                    msem = "vS" if eng == "vector" else "gS"
                    mm_ = prog.emit(eng, lambda v, si=si, w_=w_, dl=dl:
                                    v.tensor_mul(PRB[si][:, 0:w_], PRB[si][:, 0:w_],
                                                 CB("BIGM", cols=slice(dl, dl + w_))),
                                    waits=[("sS", sexp)], inc=msem)
                    wpv = (pv_free[h % 2] if j == js[0] else [])
                    tpv = prog.emit("tensor",
                                    lambda e, vb=vb, jj=jj, h=h, si=si, q0=q0, w_=w_,
                                    pvb=pvb, first=(j == js[0]):
                                    e.matmul(pvb[0:65, q0:q0 + w_], vb[:, jj, h, :],
                                             PRB[si][:, 0:w_], start=first, stop=False),
                                    waits=[(msem, mm_)] + wpv, inc="tS")
                    r_prb.release(si, "tS", tpv)
                tpa = prog.at("tS")
                sat = prog.emit("scalar", lambda s, pvb=pvb, r0=r0, ch=ch:
                                s.activation(ATT1[r0:r0 + 64, T * ch:T * (ch + 1)],
                                             pvb[0:64, :], AF.Copy),
                                waits=[("tS", tpa)], inc="sS")
                prog.emit("vector", lambda v, pvb=pvb:
                          v.tensor_scalar(misc32[64:65, :], pvb[64:65, :], 1.0, 1e-6,
                                          ALU.mult, ALU.add), waits=[("tS", tpa)])
                vrc = prog.emit("vector", lambda v:
                                v.reciprocal(misc32[64:65, :], misc32[64:65, :]),
                                inc="vS")
                tbc = prog.emit("tensor", lambda e, bcb=bcb:
                                e.matmul(bcb[0:64, :],
                                         C3("ONE32", rows=slice(64, 65),
                                            cols=slice(0, 64)),
                                         misc32[64:65, :], start=True, stop=True),
                                waits=[("vS", vrc)] + bc_free[h % 2], inc="tS")
                amv = prog.emit("vector", lambda v, bcb=bcb, r0=r0, ch=ch:
                                v.tensor_mul(ATT1[r0:r0 + 64, T * ch:T * (ch + 1)],
                                             ATT1[r0:r0 + 64, T * ch:T * (ch + 1)],
                                             bcb[0:64, :]),
                                waits=[("tS", tbc), ("sS", sat)], inc="vS")
                pv_free[h % 2] = [("vS", amv)]
                bc_free[h % 2] = [("vS", amv)]
                attn_v = amv

            if t == 0:
                for o in range(C):
                    for i in range(C):
                        prog.emit("tensor", lambda e, o=o, i=i:
                                  e.matmul(P[7][:, o:o + 1],
                                           WS[slot_of["WoT"]][:, i,
                                                              128 * o:128 * (o + 1)],
                                           CB("BVBF", cols=slice(i, i + 1)),
                                           start=(o == 0 and i == 0), stop=False),
                                  waits=([("dW", ld_tick["WoT"])]
                                         if (o == 0 and i == 0) else []),
                                  inc=("tS" if (o == C - 1 and i == C - 1) else None))
                prog.emit("vector", lambda v:
                          v.tensor_add(C3("BALL"), C3("BALL"), P[7][:, 0:8]),
                          waits=prog.now("tS"), inc="vS")
            ball_v = prog.at("vS")

            wm = [("vS", attn_v), ("vS", ball_v)] + list(mg_prev) + prog.now("sS")
            for o in range(C):
                for i in range(C):
                    prog.emit("tensor", lambda e, o=o, i=i:
                              e.matmul(P[o][:, :],
                                       WS[slot_of["WoT"]][:, i, 128 * o:128 * (o + 1)],
                                       _ch(ATT1, i), start=(i == 0), stop=False),
                              waits=(wm if (o == 0 and i == 0) else []),
                              inc=("tS" if i == C - 1 else None))
            group_done("WoT")
            for o in range(C):
                prog.emit("tensor", lambda e, o=o:
                          e.matmul(P[o][:, :], IDB, xnc(o), start=False, stop=False))
                prog.emit("tensor", lambda e, o=o:
                          e.matmul(P[o][:, :],
                                   CB("BR", rows=slice(0, 16),
                                      cols=slice(128 * o, 128 * o + 128)),
                                   g_sb[0:16, :], start=False, stop=False),
                          inc="tS")
            for nm2 in ("ws0", "ws1"):
                for o in range(C):
                    for i in range(C):
                        prog.emit("tensor", lambda e, o=o, i=i, nm2=nm2:
                                  e.matmul(P[o][:, :],
                                           WS[slot_of[nm2]][:, i, 128 * o:128 * (o + 1)],
                                           arc(XH, i), start=False, stop=False),
                                  waits=([("dW", ld_tick[nm2]), ("vS", vxh)]
                                         if (o == 0 and i == 0) else []),
                                  inc=("tS" if i == C - 1 else None))
                group_done(nm2)
            gb_dma = {}
            for e_ in range(2):
                gb_dma[e_] = prog.emit(
                    "scalar", lambda s, e_=e_:
                    s.dma_start(GB[e_ % 2], bass.AP(s_g, e_ * T, [[0, 128], [1, T]])),
                    waits=[("dR", gdma)] + gb_free[e_ % 2], inc="dR", by=16)
            for e_ in range(NE):
                for i in range(C):
                    eng = "vector" if i % 2 == 0 else "gpsimd"
                    msem = "vS" if eng == "vector" else "gS"
                    si, w_zu = r_zu.acquire()
                    zv = prog.emit(eng, lambda v, i=i, si=si, e_=e_:
                                   v.tensor_mul(ZU[si], xnc(i), GB[e_ % 2]),
                                   waits=[("dR", gb_dma[e_])] + w_zu, inc=msem)
                    tz = 0
                    for o in range(C):
                        tz = prog.emit(
                            "tensor", lambda e, o=o, i=i, si=si, e_=e_:
                            e.matmul(P[o][:, :],
                                     WS[slot_of[f"e{e_}"]][:, i, 128 * o:128 * (o + 1)],
                                     ZU[si], start=False, stop=False),
                            waits=([("dW", ld_tick[f"e{e_}"]), (msem, zv)]
                                   if o == 0 else []),
                            inc=("tS" if o == C - 1 else None))
                    r_zu.release(si, "tS", tz)
                gb_free[e_ % 2] = prog.now("tS")
                group_done(f"e{e_}")
                if e_ + 2 < NE:
                    gb_dma[e_ + 2] = prog.emit(
                        "scalar", lambda s, e_=e_:
                        s.dma_start(GB[e_ % 2],
                                    bass.AP(s_g, (e_ + 2) * T, [[0, 128], [1, T]])),
                        waits=gb_free[e_ % 2], inc="dR", by=16)
            tmega = prog.at("tS")
            for o in range(C):
                si, w_o = r_ost.acquire()
                prog.emit("vector", lambda v, o=o, si=si, t=t:
                          v.scalar_tensor_tensor(ost[si][:, :], P[o][:, :],
                                                 C3("BALL", cols=slice(o, o + 1)),
                                                 xc(o, t), ALU.add, ALU.add),
                          waits=[("tS", tmega)] + w_o, inc="vS")
                vos = prog.now("vS")
                dt_ = prog.emit("sync", lambda sy, o=o, si=si, t=t:
                                sy.dma_start(
                                    bass.AP(out_d, o * 128 * S + T * t,
                                            [[S, 128], [1, T]]), ost[si][:, :]),
                                waits=vos, inc="dO", by=16)
                r_ost.release(si, "dO", dt_)
            mg_prev = prog.now("vS")
            rope_used = [("vS", vk)]
            p01_w = prog.now("vS")
            p23_w = prog.now("vS")
            ws_free = [prog.now("tS"), prog.now("tS")]
            if stop_after == "l1t0":
                break

        _finish(nc, prog, dbg, dbg_d, locals())
    return nc, din, out_d, dbg_d


def _ch(buf, c):
    if len(buf.shape) == 3:
        return buf[:, c, :]
    return buf[:, T * c:T * (c + 1)]


def _finish(nc, prog, dbg, dbg_d, env):
    for nm in dbg:
        ten = env.get(nm)
        if ten is None:
            continue
        shp = list(ten.shape)
        flat = [shp[0], int(np.prod(shp[1:]))]
        dd = nc.dram_tensor("dbg_" + nm, flat, F32, kind="ExternalOutput")
        dbg_d[nm] = dd
        src = ten[tuple(slice(None) for _ in shp)]
        if len(shp) > 2:
            src = src.rearrange("p " + " ".join(f"a{i}" for i in range(len(shp) - 1))
                                + " -> p (" + " ".join(f"a{i}" for i in range(len(shp) - 1))
                                + ")")
        prog.emit("gpsimd", lambda g, dd=dd, src=src: g.dma_start(dd[:, :], src),
                  waits=prog.now("vS", "sS", "tS", "gS"), inc="dG", by=16)
    prog.emit("sync", lambda sy: sy.nop(),
              waits=prog.now("dO", "dG", "dW", "dR"))
    prog.replay_block()



def _host_consts(inputs):
    f32 = np.float32
    cbf = np.zeros((128, CBF_W), f32)
    c32 = np.zeros((128, C32_W), f32)

    def put_bf(name, arr):
        a, _ = CBF_COLS[name]
        cbf[:arr.shape[0], a:a + arr.shape[1]] = arr

    def put_32(name, arr):
        a, _ = C32_COLS[name]
        c32[:arr.shape[0], a:a + arr.shape[1]] = arr

    hneg = np.zeros((16, 1024), f32)
    for h in range(16):
        hneg[h, 64 * h:64 * (h + 1)] = -1.0
    put_bf("HNEG", hneg)
    i_ = np.arange(128)[:, None]
    p_ = np.arange(640)[None, :]
    put_bf("BIGM", ((p_ >= i_) & (p_ <= i_ + 512)).astype(f32))
    rf = np.asarray(inputs["L0_attn_random_features"], f32)
    rb = np.zeros((128, 128), f32)
    rb[0:64, 0:64] = rf.T
    rb[64:128, 64:128] = rf.T
    put_bf("RFBD", rb)
    put_bf("IDB", np.eye(128, dtype=f32))
    put_bf("ONE", np.ones((128, 128), f32))
    put_bf("BR", np.asarray(inputs["L1_ffn_br"], f32))
    centT = np.asarray(inputs["L1_ffn_centroids"], f32).T
    put_bf("CENT", centT.reshape(C, 128, 16).transpose(1, 0, 2).reshape(128, C * 16))
    hsel = np.zeros((128, 128), f32)
    for c in range(C):
        for p in range(128):
            hsel[p, 16 * c + 2 * c + p // 64] = 1.0
    put_bf("HSEL", hsel)
    put_bf("BVBF", np.asarray(inputs["L1_attn_bv"], f32).reshape(C, 128).T)
    selh = np.zeros((16, 2048), f32)
    for h in range(16):
        selh[h, 128 * h:128 * (h + 1)] = -1.0
    put_bf("SELH", selh)

    put_32("ID32", np.eye(128, dtype=f32))

    def pp(vec):
        return np.asarray(vec, f32).reshape(C, 128).T

    put_32("MUR", pp(inputs["L0_ffn_mu_r"]))
    put_32("MUK", pp(inputs["L0_ffn_mu_k"]))
    put_32("BOUT", pp(inputs["L0_attn_b_out"]))
    put_32("BQ", pp(inputs["L1_attn_bq"]))
    put_32("BK", pp(inputs["L1_attn_bk"]))
    put_32("BO", pp(inputs["L1_attn_bo"]))
    put_32("BS0", pp(inputs["L1_ffn_bs"][0]))
    put_32("BS1", pp(inputs["L1_ffn_bs"][1]))
    put_32("EB", np.asarray(inputs["L1_ffn_expert_biases"], f32).reshape(16, 1))
    put_32("EBB", np.broadcast_to(
        np.asarray(inputs["L1_ffn_expert_biases"], f32)[None, :], (128, 16)).copy())
    put_32("ONE32", np.ones((128, 128), f32))
    put_32("EPS5", np.full((128, 1), 1e-5, f32))
    put_32("EPS30", np.full((128, 1), 1e-30, f32))
    return cbf, c32


def _host_rope():
    f32 = np.float32
    half = D // 2
    inv = 1.0 / (10000.0 ** (np.arange(half, dtype=f32) / half))
    n = np.arange(S, dtype=f32)
    ang = inv[:, None] * n[None, :]
    cos, sin = np.cos(ang).astype(f32), np.sin(ang).astype(f32)
    out = np.zeros((NT, 128, C * T), f32)
    for t in range(NT):
        for c in range(4):
            out[t, :, T * c:T * (c + 1)] = cos[128 * c:128 * (c + 1), T * t:T * (t + 1)]
            out[t, :, T * (4 + c):T * (5 + c)] = \
                sin[128 * c:128 * (c + 1), T * t:T * (t + 1)]
    return out


def make_inputs(inputs):
    f32 = np.float32
    x = np.asarray(inputs["L0_attn_x"], f32)
    cbf, c32 = _host_consts(inputs)
    rope = _host_rope()

    def tr(a):
        return np.ascontiguousarray(np.asarray(a, f32).T)

    shared = dict(
        wqkvT=tr(inputs["L0_attn_w_qkv"]), woutT=tr(inputs["L0_attn_w_out"]),
        WrT=tr(inputs["L0_ffn_W_r"]), WkT=tr(inputs["L0_ffn_W_k"]),
        WvT=tr(inputs["L0_ffn_W_v"]), Wq1T=tr(inputs["L1_attn_Wq"]),
        Wk1T=tr(inputs["L1_attn_Wk"]), Wv1T=tr(inputs["L1_attn_Wv"]),
        WoT=tr(inputs["L1_attn_Wo"]),
        WsT=np.ascontiguousarray(
            np.asarray(inputs["L1_ffn_Ws"], f32).transpose(0, 2, 1)),
        WeT=np.ascontiguousarray(
            np.asarray(inputs["L1_ffn_Wr"], f32).transpose(0, 2, 1)),
        ropein=rope, cbf=cbf, c32=c32,
    )
    in_maps = []
    for b in range(8):
        m = dict(shared)
        m["x_fm"] = np.ascontiguousarray(x[b].T)
        in_maps.append(m)
    return in_maps


def kernel(**inputs):
    nc, din, out_d, dbg_d = build_nc(serial=True)
    in_maps = make_inputs(inputs)
    res = run_bass_kernel_spmd(nc, in_maps, core_ids=list(range(8)))
    outs = [np.ascontiguousarray(res.results[b]["out_fm"].T) for b in range(8)]
    return np.stack(outs, axis=0)


# version 02 (final)
# speedup vs baseline: 1.0003x; 1.0003x over previous
import numpy as np
import ml_dtypes
from contextlib import ExitStack

import concourse.bass as bass
import concourse.mybir as mybir
from concourse.bass_utils import run_bass_kernel_spmd

F32 = mybir.dt.float32
BF16 = mybir.dt.bfloat16
AF = mybir.ActivationFunctionType
ALU = mybir.AluOpType
AX = mybir.AxisListType

D, C, S, T, H, NE = 1024, 8, 4096, 512, 16, 16
NT = S // T

W_J = [128, 256, 384, 512, 512, 384, 256, 128]
Q0_J = [0, 0, 0, 0, 0, 128, 256, 384]
DLT_J = [512, 384, 256, 128, 0, 0, 0, 0]


def _cols(widths):
    out, a = {}, 0
    for name, w in widths:
        out[name] = (a, a + w)
        a += w
    return out, a


CBF_COLS, CBF_W = _cols([
    ("HNEG", 1024), ("BIGM", 640), ("RFBD", 128), ("IDB", 128), ("ONE", 128),
    ("BR", 1024), ("CENT", 128), ("HSEL", 128), ("BVBF", 8), ("SELH", 2048),
])
C32_COLS, C32_W = _cols([
    ("ID32", 128), ("MUR", 8), ("MUK", 8), ("BOUT", 8), ("BQ", 8), ("BK", 8),
    ("BO", 8), ("BS0", 8), ("BS1", 8), ("EB", 1), ("BALL", 8), ("ONE32", 128), ("EPS5", 1), ("EPS30", 1), ("EBB", 16),
])


class Prog:
    serial = False

    def __init__(self, nc, ctx):
        self.nc = nc
        self.items = {k: [] for k in ("tensor", "vector", "scalar", "gpsimd", "sync")}
        self.sems, self.counts = {}, {}
        self.ctx = ctx

    def sem(self, name):
        self.sems[name] = self.ctx.enter_context(self.nc.semaphore(name))
        self.counts[name] = 0

    def at(self, name):
        return self.counts[name]

    def now(self, *names):
        return [(n, self.counts[n]) for n in names]

    def emit(self, engine, fn, waits=(), inc=None, by=1):
        tgt = None
        if self.serial:
            waits = list(waits) + [(k, v) for k, v in self.counts.items()]
        if inc is not None:
            self.counts[inc] += by
            tgt = self.counts[inc]
        waits = tuple((s, v) for (s, v) in waits if v > 0)

        def item(eng, fn=fn, waits=waits, inc=inc, by=by):
            for (s, v) in waits:
                eng.wait_ge(self.sems[s], v)
            ins = fn(eng)
            if inc is not None:
                ins.then_inc(self.sems[inc], by)

        self.items[engine].append(item)
        return tgt

    def replay_block(self):
        with self.nc.Block() as blk:
            for name in ("sync", "gpsimd", "tensor", "scalar", "vector"):
                fns = list(self.items[name])
                if not fns:
                    continue

                def runner(eng, fns=fns):
                    for f in fns:
                        f(eng)

                getattr(blk, name)(runner)
        self.items = {k: [] for k in self.items}


class Rotor:

    def __init__(self, n):
        self.free = [[] for _ in range(n)]
        self.n = n
        self.i = 0

    def acquire(self):
        s = self.i % self.n
        self.i += 1
        return s, list(self.free[s])

    def release(self, s, sem, count):
        self.free[s] = [(sem, count)]


def build_nc(stop_after=None, dbg=(), serial=False, loose=()):
    nc = bass.Bass("TRN2", target_bir_lowering=False)
    din = {}

    def dram_in(name, shape):
        din[name] = nc.dram_tensor(name, list(shape), F32, kind="ExternalInput")
        return din[name]

    x_in = dram_in("x_fm", (D, S))
    for nm, sh in [("wqkvT", (D, 3 * D)), ("woutT", (D, D)), ("WrT", (D, D)),
                   ("WkT", (D, D)), ("WvT", (D, D)), ("Wq1T", (D, D)),
                   ("Wk1T", (D, D)), ("Wv1T", (D, D)), ("WoT", (D, D)),
                   ("WsT", (2, D, D)), ("WeT", (NE, D, D)),
                   ("ropein", (NT, 128, C * T)), ("cbf", (128, CBF_W)),
                   ("c32", (128, C32_W))]:
        dram_in(nm, sh)
    out_d = nc.dram_tensor("out_fm", [D, S], F32, kind="ExternalOutput")

    sc = {}
    for name, shape in [("wqkvT", [D, 3 * D]), ("woutT", [D, D]), ("WrT", [D, D]),
                        ("WkT", [D, D]), ("WvT", [D, D]), ("Wq1T", [D, D]),
                        ("Wk1T", [D, D]), ("Wv1T", [D, D]), ("WoT", [D, D]),
                        ("WsT", [2, D, D]), ("WeT", [NE, D, D]),
                        ("rope", [NT, 128, C * T])]:
        sc[name] = nc.dram_tensor("s_" + name, shape, BF16)
    s_g = nc.dram_tensor("s_g", [16, T], BF16)

    dbg_d = {}
    with ExitStack() as ctx:
        E = ctx.enter_context
        x_sb = E(nc.sbuf_tensor("x_sb", [128, C, S], BF16))
        xn_sb = E(nc.sbuf_tensor("xn_sb", [128, C, T + 1], BF16))
        WS = [E(nc.sbuf_tensor(f"WS{i}", [128, C, D], BF16)) for i in range(3)]
        ws2_off = nc.lookup_mloc(WS[2]).addr
        k_bufs = [nc.alloc_sbuf_tensor_at("kb0", [128, C, T], BF16, offset=ws2_off),
                  nc.alloc_sbuf_tensor_at("kb1", [128, C, T], BF16,
                                          offset=ws2_off + C * T * 2)]
        v_bufs = [E(nc.sbuf_tensor(f"vtm{i}", [128, 4, H, 65], BF16)) for i in range(2)]
        AR1 = E(nc.sbuf_tensor("AR1", [128, 4096], BF16))
        AR2 = E(nc.sbuf_tensor("AR2", [128, 4096], BF16))
        AR3 = E(nc.sbuf_tensor("AR3", [128, 4096], BF16))
        AR4 = E(nc.sbuf_tensor("AR4", [128, 4096], BF16))
        sqbuf = E(nc.sbuf_tensor("sqbuf", [128, 1024], BF16))
        kvacc32 = nc.alloc_sbuf_tensor_at(
            "kvacc32", [128, 1040], F32, offset=nc.lookup_mloc(AR3).addr)
        rope_sb = E(nc.sbuf_tensor("rope_sb", [128, C, T], BF16))
        cbf = E(nc.sbuf_tensor("cbf_sb", [128, CBF_W], BF16))
        c32 = E(nc.sbuf_tensor("c32_sb", [128, C32_W], F32))
        stat32 = E(nc.sbuf_tensor("stat32", [128, 512], F32))
        misc32 = E(nc.sbuf_tensor("misc32", [128, 512], F32))
        gt32 = E(nc.sbuf_tensor("gt32", [128, 80], F32))
        mk2 = E(nc.sbuf_tensor("mk2", [16, 16], F32))
        ksum32 = E(nc.sbuf_tensor("ksum32", [128, 8], F32))
        kv_sb = E(nc.sbuf_tensor("kv_sb", [128, C, 128], BF16))
        ksd = E(nc.sbuf_tensor("ksd", [128, 128], BF16))
        mhat = E(nc.sbuf_tensor("mhat", [16, 512], BF16))
        g_sb = E(nc.sbuf_tensor("g_sb", [16, 512], BF16))
        xt32 = [E(nc.sbuf_tensor(f"xt32_{i}", [128, 512], F32)) for i in range(2)]
        ost = [E(nc.sbuf_tensor(f"ost{i}", [128, 512], F32)) for i in range(2)]
        gdump = E(nc.sbuf_tensor("gdump", [128, 64], F32)) if "gdump" in dbg else None
        P = [E(nc.psum_tensor(f"P{i}", [128, 512], F32)) for i in range(8)]

        def CBo(name):
            return CBF_COLS[name][0]

        def CB(name, rows=slice(0, 128), cols=None):
            a, b = CBF_COLS[name]
            if cols is not None:
                b = a + cols.stop
                a = a + cols.start
            return cbf[rows, a:b]

        def C3(name, rows=slice(0, 128), cols=None):
            a, b = C32_COLS[name]
            if cols is not None:
                b = a + cols.stop
                a = a + cols.start
            return c32[rows, a:b]

        ONEc = CB("ONE", cols=slice(0, 1))
        ONEr = C3("ONE32", rows=slice(0, 1))
        RFBD = CB("RFBD")
        IDB = CB("IDB")

        prog = Prog(nc, ctx)
        prog.serial = serial and 'setup' not in loose
        for sname in ("dG", "dW", "dR", "dO", "tS", "vS", "sS", "gS"):
            prog.sem(sname)

        def xc(c, t):
            return x_sb[:, c, T * t:T * (t + 1)]

        def xnc(c):
            return xn_sb[:, c, 1:T + 1]

        def xnp(c):
            return xn_sb[:, c, 0:T]

        def arc(buf, c):
            return buf[:, T * c:T * (c + 1)]

        def artm(buf, nb):
            return buf[:, 1024 * nb:1024 * (nb + 1)]

        SQ = [sqbuf[:, 512 * i:512 * (i + 1)] for i in range(2)]

        def d_gp(fn):
            return prog.emit("gpsimd", fn, inc="dG", by=16)

        d_gp(lambda g: g.dma_start(cbf[:, :], din["cbf"][:, :]))
        d_gp(lambda g: g.dma_start(c32[:, :], din["c32"][:, :]))
        d_gp(lambda g: g.dma_start(
            x_sb[:, :, :], bass.AP(din["x_fm"], 0, [[S, 128], [128 * S, C], [1, S]])))
        for nm in ("wqkvT", "woutT", "WrT", "WkT", "WvT"):
            d_gp(lambda g, nm=nm: g.dma_start(sc[nm][:, :], din[nm][:, :]))
        su = prog.at("dG")
        prog.emit("vector", lambda v: v.tensor_scalar_mul(RFBD, RFBD, 0.125),
                  waits=[("dG", su)], inc="vS")
        prog.emit("vector", lambda v: v.tensor_add(stat32[:, 0:8], C3("BS0"), C3("BS1")))
        prog.emit("vector", lambda v: v.tensor_scalar(
            stat32[:, 0:8], stat32[:, 0:8], 0.5, None, ALU.mult))
        prog.emit("vector", lambda v: v.tensor_add(C3("BALL"), stat32[:, 0:8], C3("BO")))
        prog.emit("vector", lambda v: v.memset(mk2[:, :], -1e30))
        for b in range(2):
            prog.emit("vector", lambda v, b=b: v.memset(v_bufs[b][:, :, :, 64:65], 1.0),
                      inc="vS")
        prog.replay_block()

        conv = {}
        for nm in ("Wq1T", "Wk1T", "Wv1T", "WoT"):
            conv[nm] = d_gp(lambda g, nm=nm: g.dma_start(sc[nm][:, :], din[nm][:, :]))
        for e2 in range(2):
            conv[f"ws{e2}"] = d_gp(
                lambda g, e2=e2: g.dma_start(sc["WsT"][e2, :, :], din["WsT"][e2, :, :]))
        for e in range(NE):
            conv[f"e{e}"] = d_gp(
                lambda g, e=e: g.dma_start(sc["WeT"][e, :, :], din["WeT"][e, :, :]))
        for t in range(NT):
            conv[f"rope{t}"] = d_gp(
                lambda g, t=t: g.dma_start(sc["rope"][t, :, :], din["ropein"][t, :, :]))

        def load_w(slot, src, col0, waits=()):
            ncols = src.shape[-1]

            def fn(sy, slot=slot, src=src, col0=col0, ncols=ncols):
                return sy.dma_start(
                    slot[:, :, :],
                    bass.AP(src, col0, [[ncols, 128], [128 * ncols, C], [1, D]]))

            return prog.emit("sync", fn, waits=waits, inc="dW", by=16)

        def load_w3(slot, src3, idx, waits=()):
            def fn(sy, slot=slot, src3=src3, idx=idx):
                return sy.dma_start(
                    slot[:, :, :],
                    bass.AP(src3, idx * D * D, [[D, 128], [128 * D, C], [1, D]]))

            return prog.emit("sync", fn, waits=waits, inc="dW", by=16)

        r_sq = Rotor(2)
        r_xt = Rotor(2)
        r_ost = Rotor(2)

        def emit_ln(t, p01_w, p23_w, xn_w=()):
            nm_ = stat32[0:1, 0:512]
            istd = misc32[0:1, 0:512]
            tmpA = stat32[32:33, 0:512]
            tk = 0
            for c in range(C):
                si, w_sq = r_sq.acquire()
                prog.emit("scalar", lambda s, c=c, si=si, t=t:
                          s.activation(SQ[si], xc(c, t), AF.Square),
                          waits=w_sq, inc="sS")
                s_now = prog.now("sS")
                prog.emit("tensor", lambda e, c=c, t=t:
                          e.matmul(P[0][0:1, :], ONEc, xc(c, t),
                                   start=(c == 0), stop=(c == C - 1)),
                          waits=(list(p01_w) if c == 0 else []))
                tk = prog.emit("tensor", lambda e, c=c, si=si:
                               e.matmul(P[1][0:1, :], ONEc, SQ[si],
                                        start=(c == 0), stop=(c == C - 1)),
                               waits=s_now, inc="tS")
                r_sq.release(si, "tS", tk)
            prog.emit("vector", lambda v:
                      v.tensor_scalar(nm_, P[0][0:1, :], -1.0 / D, None, ALU.mult),
                      waits=[("tS", tk)])
            prog.emit("vector", lambda v: v.tensor_mul(tmpA, nm_, nm_))
            prog.emit("vector", lambda v:
                      v.scalar_tensor_tensor(tmpA, P[1][0:1, :], 1.0 / D, tmpA,
                                             ALU.mult, ALU.subtract), inc="vS")
            vv = prog.now("vS")
            prog.emit("scalar", lambda s: s.activation(tmpA, tmpA, AF.Ln, bias=C3("EPS5", rows=slice(32, 33))),
                      waits=vv)
            prog.emit("scalar", lambda s: s.activation(istd, tmpA, AF.Exp, scale=-0.5),
                      inc="sS")
            sis = prog.now("sS")
            prog.emit("tensor", lambda e:
                      e.matmul(P[2][:, :], ONEr, nm_, start=True, stop=True),
                      waits=sis + list(p23_w))
            tb = prog.emit("tensor", lambda e:
                           e.matmul(P[3][:, :], ONEr, istd, start=True, stop=True),
                           inc="tS")
            for c in range(C):
                si, w_xt = r_xt.acquire()
                prog.emit("vector", lambda v, c=c, si=si, t=t:
                          v.tensor_add(xt32[si][:, :], xc(c, t), P[2][:, :]),
                          waits=[("tS", tb)] + w_xt + (list(xn_w) if c == 0 else []))
                vk = prog.emit("vector", lambda v, c=c, si=si:
                               v.tensor_mul(xnc(c), xt32[si][:, :], P[3][:, :]), inc="vS")
                r_xt.release(si, "vS", vk)
            return prog.at("vS")

        prog.serial = serial and "l0p1" not in loose
        wk_ld = load_w(WS[0], sc["wqkvT"], D, waits=[("dG", su)])
        wv_ld = load_w(WS[1], sc["wqkvT"], 2 * D, waits=[("dG", su)])
        wq_ld = load_w(WS[2], sc["wqkvT"], 0, waits=[("dG", su)])

        KFM, KTM, VTM = AR1, AR2, AR4
        kv_freed = [[]]
        p01_w, p23_w = [], []
        for t in range(NT):
            prog.serial = serial and "ln" not in loose
            vxn = emit_ln(t, p01_w, p23_w)
            prog.serial = serial and "kproj" not in loose
            ev = [[], []]
            for o in range(C):
                pb = P[o % 2]
                w0 = [("vS", vxn)] + ev[o % 2] + ([("dW", wk_ld)] if t == 0 and o == 0 else [])
                tk = 0
                for i in range(C):
                    tk = prog.emit("tensor", lambda e, o=o, i=i, pb=pb:
                                   e.matmul(pb[:, :], WS[0][:, i, 128 * o:128 * (o + 1)],
                                            xnc(i), start=(i == 0), stop=(i == C - 1)),
                                   waits=(w0 if i == 0 else []),
                                   inc=("tS" if i == C - 1 else None))
                se = prog.emit("scalar", lambda s, o=o, pb=pb:
                               s.activation(arc(KFM, o), pb[:, :], AF.Copy),
                               waits=[("tS", tk)], inc="sS")
                ev[o % 2] = [("sS", se)]
            for c in range(C):
                si, w_sq = r_sq.acquire()
                prog.emit("scalar", lambda s, c=c, si=si:
                          s.activation(SQ[si], arc(KFM, c), AF.Square), waits=w_sq,
                          inc="sS")
                s_now = prog.now("sS")
                tk = prog.emit("tensor", lambda e, c=c, si=si:
                               e.matmul(P[4][0:16, :],
                                        CB("HSEL", cols=slice(16 * c, 16 * c + 16)),
                                        SQ[si], start=(c == 0), stop=(c == C - 1)),
                               waits=s_now, inc="tS")
                r_sq.release(si, "tS", tk)
            prog.emit("scalar", lambda s:
                      s.activation(misc32[0:16, :], P[4][0:16, :], AF.Ln, bias=C3("EPS30", rows=slice(0, 16))),
                      waits=[("tS", tk)])
            skn = prog.emit("scalar", lambda s:
                            s.activation(mhat[:, :], misc32[0:16, :], AF.Exp, scale=0.5),
                            inc="sS")
            vkn = prog.emit("vector", lambda v: v.engine_nop(),
                            waits=[("sS", skn)], inc="vS")
            ktm_ev, vtm_ev = [], []
            for nb in range(4):
                prog.serial = serial and "projtm" not in loose
                w0 = [("vS", vkn)] + ktm_ev
                for c in range(C):
                    pb = P[2 + c // 4]
                    cols = slice(128 * (c % 4), 128 * (c % 4) + 128)
                    prog.emit("tensor", lambda e, nb=nb, c=c, pb=pb, cols=cols:
                              e.matmul(pb[:, cols],
                                       KFM[:, T * c + 128 * nb:T * c + 128 * (nb + 1)],
                                       RFBD, start=True, stop=False),
                              waits=(w0 if c == 0 else []))
                    prog.emit("tensor", lambda e, nb=nb, c=c, pb=pb, cols=cols:
                              e.matmul(pb[:, cols], mhat[0:16, 128 * nb:128 * (nb + 1)],
                                       CB("HNEG", rows=slice(0, 16),
                                          cols=slice(128 * c, 128 * c + 128)),
                                       start=False, stop=True),
                              inc=("tS" if c in (3, 7) else None))
                tpt = prog.at("tS")
                ktm_ev = []
                for hf in range(2):
                    se = prog.emit("scalar", lambda s, nb=nb, hf=hf:
                                   s.activation(artm(KTM, nb)[:, 512 * hf:512 * (hf + 1)],
                                                P[2 + hf][:, :], AF.Exp),
                                   waits=[("tS", tpt)], inc="sS")
                    ktm_ev = [("sS", se)]
                prog.serial = serial and "vproj" not in loose
                w1 = list(vtm_ev) + ([("dW", wv_ld)] if (t == 0 and nb == 0) else [])
                if nb == 0:
                    w1 += ev[0] + ev[1]
                for i in range(C):
                    for hf in range(2):
                        prog.emit("tensor", lambda e, nb=nb, i=i, hf=hf:
                                  e.matmul(P[hf][:, :],
                                           xn_sb[:, i, 1 + 128 * nb:1 + 128 * (nb + 1)],
                                           WS[1][:, i, 512 * hf:512 * (hf + 1)],
                                           start=(i == 0), stop=(i == C - 1)),
                                  waits=(w1 if (i == 0 and hf == 0) else []),
                                  inc=("tS" if (i == C - 1 and hf == 1) else None))
                tv = prog.at("tS")
                vtm_ev = []
                for hf in range(2):
                    se = prog.emit("scalar", lambda s, nb=nb, hf=hf:
                                   s.activation(artm(VTM, nb)[:, 512 * hf:512 * (hf + 1)],
                                                P[hf][:, :], AF.Copy),
                                   waits=[("tS", tv)], inc="sS")
                    vtm_ev = [("sS", se)]
            prog.serial = serial and "kvacc" not in loose
            sv = prog.now("sS")
            kvw = list(kv_freed[0])
            kvadds = []
            for nb in range(4):
                for c in range(C):
                    bank, cc = c // 4, c % 4
                    cols = slice(128 * cc, 128 * cc + 128)
                    prog.emit("tensor",
                              lambda e, nb=nb, c=c, bank=bank, cols=cols:
                              e.matmul(P[5 + bank][:, cols],
                                       artm(KTM, nb)[:, 128 * c:128 * (c + 1)],
                                       artm(VTM, nb)[:, 128 * c:128 * (c + 1)],
                                       start=(c in (0, 4)), stop=(c in (3, 7))),
                              waits=(sv + kvw if c == 0 else []),
                              inc=("tS" if c == C - 1 else None))
                tnb = prog.at("tS")
                if t == 0 and nb == 0:
                    prog.emit("vector", lambda v:
                              v.tensor_copy(kvacc32[:, 0:512], P[5][:, :]),
                              waits=[("tS", tnb)])
                    kvf = prog.emit("vector", lambda v:
                                    v.tensor_copy(kvacc32[:, 512:1024], P[6][:, :]),
                                    inc="vS")
                else:
                    prog.emit("vector", lambda v:
                              v.tensor_add(kvacc32[:, 0:512], kvacc32[:, 0:512],
                                           P[5][:, :]),
                              waits=[("tS", tnb)])
                    kvf = prog.emit("vector", lambda v:
                                    v.tensor_add(kvacc32[:, 512:1024],
                                                 kvacc32[:, 512:1024], P[6][:, :]),
                                    inc="vS")
                kvw = [("vS", kvf)]
            kv_freed[0] = kvw
            for nb in range(4):
                for c in range(C):
                    prog.emit("tensor", lambda e, nb=nb, c=c, t=t:
                              e.matmul(P[7][:, c:c + 1],
                                       artm(KTM, nb)[:, 128 * c:128 * (c + 1)],
                                       ONEc,
                                       start=(t == 0 and nb == 0 and c == 0),
                                       stop=(t == NT - 1 and nb == 3)),
                              inc=("tS" if (nb == 3 and c == C - 1) else None))
            pv = prog.emit("vector", lambda v: v.engine_nop(),
                           waits=prog.now("sS"), inc="vS")
            p01_w = [("vS", pv)]
            p23_w = [("vS", pv)]

        tall = prog.at("tS")
        vall = prog.at("vS")
        prog.emit("vector", lambda v: v.memset(kv_sb[:, :, :], 0.0),
                  waits=[("tS", tall)])
        for c in range(C):
            src0 = 512 * (c // 4) + 128 * (c % 4)
            for hh in range(2):
                sl = slice(64 * hh, 64 * hh + 64)
                prog.emit("scalar", lambda s, c=c, src0=src0, sl=sl, hh=hh:
                          s.activation(kv_sb[sl, c, 64 * hh:64 * hh + 64],
                                       kvacc32[sl, src0 + 64 * hh:src0 + 64 * hh + 64],
                                       AF.Copy),
                          waits=[("vS", vall)], inc="sS")
        prog.emit("vector", lambda v:
                  v.tensor_copy(ksum32[:, :], P[7][:, 0:8]),
                  waits=[("tS", tall)])
        prog.emit("vector", lambda v: v.memset(ksd[:, :], 0.0))
        for c in range(C):
            for hh in range(2):
                sl = slice(64 * hh, 64 * hh + 64)
                col = 16 * c + 2 * c + hh
                prog.emit("vector", lambda v, c=c, sl=sl, col=col:
                          v.tensor_copy(ksd[sl, col:col + 1], ksum32[sl, c:c + 1]),
                          inc="vS")
        mid_v = prog.at("vS")
        mid_s = prog.at("sS")

        if stop_after == "l0p1":
            _finish(nc, prog, dbg, dbg_d, locals())
            return nc, din, out_d, dbg_d

        prog.serial = serial and "l0p2" not in loose
        QP, ATT, XKB, XRB = AR1, AR2, AR3, AR4
        p01_w = [("vS", mid_v)]
        p23_w = [("vS", mid_v)]
        ws_free = [[("sS", mid_s)], [("sS", mid_s)]]
        for t in range(NT):
            wWr = load_w(WS[0], sc["WrT"], 0, waits=ws_free[0])
            wWk = load_w(WS[1], sc["WkT"], 0, waits=ws_free[1])
            gw = prog.now("gS")
            if t == 0:
                prog.emit("vector", lambda v: v.memset(xn_sb[:, :, 0:1], 0.0),
                          waits=[("vS", mid_v), ("sS", mid_s)], inc="vS")
            else:
                for c in range(C):
                    prog.emit("vector", lambda v, c=c:
                              v.tensor_copy(xn_sb[:, c, 0:1], xn_sb[:, c, T:T + 1]),
                              waits=(gw if c == 0 else []), inc="vS")
            vxn = emit_ln(t, p01_w, p23_w, xn_w=gw)
            ev = [[], []]
            tk = 0
            for o in range(C):
                pb = P[o % 2]
                w0 = [("vS", vxn)] + ev[o % 2] + ([("dW", wq_ld)] if t == 0 and o == 0 else [])
                for i in range(C):
                    tk = prog.emit("tensor", lambda e, o=o, i=i, pb=pb:
                                   e.matmul(pb[:, :], WS[2][:, i, 128 * o:128 * (o + 1)],
                                            xnc(i), start=(i == 0), stop=(i == C - 1)),
                                   waits=(w0 if i == 0 else []),
                                   inc=("tS" if i == C - 1 else None))
                se = prog.emit("scalar", lambda s, o=o, pb=pb:
                               s.activation(arc(XRB, o), pb[:, :], AF.Copy),
                               waits=[("tS", tk)], inc="sS")
                ev[o % 2] = [("sS", se)]
                si, w_sq = r_sq.acquire()
                prog.emit("scalar", lambda s, o=o, si=si:
                          s.activation(SQ[si], arc(XRB, o), AF.Square), waits=w_sq,
                          inc="sS")
                s_now = prog.now("sS")
                tq = prog.emit("tensor", lambda e, o=o, si=si:
                               e.matmul(P[4][0:16, :],
                                        CB("HSEL", cols=slice(16 * o, 16 * o + 16)),
                                        SQ[si], start=(o == 0), stop=(o == C - 1)),
                               waits=s_now, inc="tS")
                r_sq.release(si, "tS", tq)
            prog.emit("scalar", lambda s:
                      s.activation(misc32[0:16, :], P[4][0:16, :], AF.Ln, bias=C3("EPS30", rows=slice(0, 16))),
                      waits=[("tS", tq)])
            sqn = prog.emit("scalar", lambda s:
                            s.activation(mhat[:, :], misc32[0:16, :], AF.Exp, scale=0.5),
                            inc="sS")
            vqn = prog.emit("vector", lambda v: v.engine_nop(),
                            waits=[("sS", sqn)], inc="vS")
            qev = [[], []]
            tfeat = 0
            for c in range(C):
                pb = P[2 + c % 2]
                w0 = [("vS", vqn)] + qev[c % 2]
                prog.emit("tensor", lambda e, c=c, pb=pb:
                          e.matmul(pb[:, :], RFBD, arc(XRB, c), start=True, stop=False),
                          waits=w0)
                tfeat = prog.emit("tensor", lambda e, c=c, pb=pb:
                                  e.matmul(pb[:, :],
                                           CB("HNEG", rows=slice(0, 16),
                                              cols=slice(128 * c, 128 * c + 128)),
                                           mhat[:, :], start=False, stop=True), inc="tS")
                se = prog.emit("scalar", lambda s, c=c, pb=pb:
                               s.activation(arc(QP, c), pb[:, :], AF.Exp),
                               waits=[("tS", tfeat)], inc="sS")
                qev[c % 2] = [("sS", se)]
            sqp = prog.at("sS")
            aev = [[("sS", sqp)], [("sS", sqp)]]
            tden = 0
            for c in range(C):
                pb = P[c % 2]
                prog.emit("tensor", lambda e, c=c, pb=pb:
                          e.matmul(pb[:, :], kv_sb[:, c, :], arc(QP, c),
                                   start=True, stop=True), waits=aev[c % 2])
                tden = prog.emit("tensor", lambda e, c=c:
                                 e.matmul(P[4][0:16, :], ksd[:, 16 * c:16 * (c + 1)],
                                          arc(QP, c), start=(c == 0), stop=(c == C - 1)),
                                 inc="tS")
                se = prog.emit("scalar", lambda s, c=c, pb=pb:
                               s.activation(arc(ATT, c), pb[:, :], AF.Copy),
                               waits=[("tS", tden)], inc="sS")
                aev[c % 2] = [("sS", se)]
            prog.emit("vector", lambda v:
                      v.tensor_scalar(misc32[0:16, :], P[4][0:16, :], -1.0, -1e-6,
                                      ALU.mult, ALU.add), waits=[("tS", tden)])
            prog.emit("vector", lambda v:
                      v.reciprocal(misc32[0:16, :], misc32[0:16, :]))
            vrd = prog.emit("vector", lambda v: v.tensor_copy(mhat[:, :], misc32[0:16, :]),
                            inc="vS")
            amul = [[("vS", vrd)], [("vS", vrd)]]
            for c in range(C):
                pb = P[5 + c % 2]
                tb2 = prog.emit("tensor", lambda e, c=c, pb=pb:
                                e.matmul(pb[:, :],
                                         CB("HNEG", rows=slice(0, 16),
                                            cols=slice(128 * c, 128 * c + 128)),
                                         mhat[:, :], start=True, stop=True),
                                waits=amul[c % 2] + aev[c % 2], inc="tS")
                av = prog.emit("vector", lambda v, c=c, pb=pb:
                               v.tensor_mul(arc(ATT, c), arc(ATT, c), pb[:, :]),
                               waits=[("tS", tb2)], inc="vS")
                amul[c % 2] = [("vS", av)]
            vatt = prog.at("vS")
            gP = 0
            for c in range(C):
                w0 = [("sS", sqp), ("tS", tden)] if c == 0 else []
                prog.emit("gpsimd", lambda g, c=c:
                          g.tensor_sub(arc(XRB, c), xnc(c), xnp(c)), waits=w0)
                gsub = prog.now("gS")
                prog.emit("gpsimd", lambda g, c=c: g.nop(), inc="gS")
                prog.emit("vector", lambda v, c=c:
                          v.scalar_tensor_tensor(arc(XKB, c), arc(XRB, c),
                                                 C3("MUK", cols=slice(c, c + 1)),
                                                 xnp(c), ALU.mult, ALU.add),
                          waits=[("gS", prog.at("gS"))])
                prog.emit("vector", lambda v, c=c:
                          v.scalar_tensor_tensor(arc(XRB, c), arc(XRB, c),
                                                 C3("MUR", cols=slice(c, c + 1)),
                                                 xnp(c), ALU.mult, ALU.add),
                          inc="vS")
            gP = prog.at("vS")
            rv = [[], []]
            for o in range(C):
                pb = P[2 + o % 2]
                w0 = ([("vS", gP), ("dW", wWr)] if o == 0 else []) + rv[o % 2]
                for i in range(C):
                    tk = prog.emit("tensor", lambda e, o=o, i=i, pb=pb:
                                   e.matmul(pb[:, :], WS[0][:, i, 128 * o:128 * (o + 1)],
                                            arc(XRB, i), start=(i == 0), stop=(i == C - 1)),
                                   waits=(w0 if i == 0 else []),
                                   inc=("tS" if i == C - 1 else None))
                si, w_xt = r_xt.acquire()
                prog.emit("scalar", lambda s, pb=pb, si=si:
                          s.activation(xt32[si][:, :], pb[:, :], AF.Exp, scale=-1.0),
                          waits=[("tS", tk)] + w_xt, inc="sS")
                s_now = prog.now("sS")
                prog.emit("vector", lambda v, si=si:
                          v.tensor_scalar(xt32[si][:, :], xt32[si][:, :], 1.0, None, ALU.add),
                          waits=s_now)
                def _recip(v, o=o, si=si):
                    with nc.allow_low_precision(reason="sigmoid gate stored bf16"):
                        return v.reciprocal(arc(QP, o), xt32[si][:, :])
                rr = prog.emit("vector", _recip, inc="vS")
                rv[o % 2] = [("vS", rr)]
                r_xt.release(si, "vS", rr)
            kv2 = [[], []]
            for o in range(C):
                pb = P[2 + o % 2]
                w0 = ([("dW", wWk)] + prog.now("vS") if o == 0 else []) + kv2[o % 2]
                for i in range(C):
                    tk = prog.emit("tensor", lambda e, o=o, i=i, pb=pb:
                                   e.matmul(pb[:, :], WS[1][:, i, 128 * o:128 * (o + 1)],
                                            arc(XKB, i), start=(i == 0), stop=(i == C - 1)),
                                   waits=(w0 if i == 0 else []),
                                   inc=("tS" if i == C - 1 else None))
                prog.emit("scalar", lambda s, o=o, pb=pb:
                          s.activation(arc(XRB, o), pb[:, :], AF.Relu),
                          waits=[("tS", tk)], inc="sS")
                se = prog.emit("scalar", lambda s, o=o:
                               s.activation(arc(XRB, o), arc(XRB, o), AF.Square),
                               inc="sS")
                kv2[o % 2] = [("sS", se)]
            skact = prog.at("sS")
            wWv = load_w(WS[0], sc["WvT"], 0, waits=prog.now("tS"))
            wWo = load_w(WS[1], sc["woutT"], 0, waits=prog.now("tS"))
            tmul = [[], []]
            for o in range(C):
                pb = P[o % 2]
                w0 = ([("dW", wWv), ("sS", skact)] if o == 0 else []) + tmul[o % 2]
                for i in range(C):
                    tk = prog.emit("tensor", lambda e, o=o, i=i, pb=pb:
                                   e.matmul(pb[:, :], WS[0][:, i, 128 * o:128 * (o + 1)],
                                            arc(XRB, i), start=(i == 0), stop=(i == C - 1)),
                                   waits=(w0 if i == 0 else []),
                                   inc=("tS" if i == C - 1 else None))
                tm = prog.emit("vector", lambda v, o=o, pb=pb:
                               v.tensor_mul(arc(QP, o), arc(QP, o), pb[:, :]),
                               waits=[("tS", tk)], inc="vS")
                tmul[o % 2] = [("vS", tm)]
            xuv = [[], []]
            for o in range(C):
                pb = P[2 + o % 2]
                w0 = ([("dW", wWo), ("vS", vatt)] if o == 0 else []) + xuv[o % 2]
                for i in range(C):
                    tk = prog.emit("tensor", lambda e, o=o, i=i, pb=pb:
                                   e.matmul(pb[:, :], WS[1][:, i, 128 * o:128 * (o + 1)],
                                            arc(ATT, i), start=(i == 0), stop=(i == C - 1)),
                                   waits=(w0 if i == 0 else []),
                                   inc=("tS" if i == C - 1 else None))
                si, w_xt = r_xt.acquire()
                prog.emit("vector", lambda v, o=o, pb=pb, si=si, t=t:
                          v.scalar_tensor_tensor(xt32[si][:, :], pb[:, :],
                                                 C3("BOUT", cols=slice(o, o + 1)),
                                                 xc(o, t), ALU.add, ALU.add),
                          waits=[("tS", tk)] + tmul[o % 2] + w_xt)
                xv = prog.emit("vector", lambda v, o=o, si=si, t=t:
                               v.tensor_add(xc(o, t), xt32[si][:, :], arc(QP, o)), inc="vS")
                xuv[o % 2] = [("vS", xv)]
                r_xt.release(si, "vS", xv)
            ws_free = [prog.now("tS"), prog.now("tS")]
            p01_w = prog.now("vS")
            p23_w = prog.now("vS")

        if stop_after == "l0":
            _finish(nc, prog, dbg, dbg_d, locals())
            return nc, din, out_d, dbg_d

        prog.serial = serial and "l1" not in loose
        QROT, ATT1, XH = AR1, AR2, AR4
        ZU = [AR3[:, 512 * i:512 * (i + 1)] for i in range(2)]
        PRB = [AR3[:, 1024 + 512 * i:1024 + 512 * (i + 1)] for i in range(2)]
        GB = [AR3[:, 2048 + 512 * i:2048 + 512 * (i + 1)] for i in range(2)]
        r_zu = Rotor(2)
        r_prb = Rotor(2)
        gb_free = [[], []]
        sg_busy = []
        mg_prev = []
        rope_used = []
        for t in range(NT):
            cur, prv = t % 2, 1 - (t % 2)
            names = ["Wq1T", "Wk1T", "Wv1T", "WoT", "ws0", "ws1"] + \
                    [f"e{e}" for e in range(NE)]
            ld_tick, slot_of = {}, {}
            ws_user_done = {0: list(ws_free[0]), 1: list(ws_free[1])}
            next_load = [0]

            def issue_load():
                idx = next_load[0]
                next_load[0] += 1
                nm = names[idx]
                sl = idx % 2
                slot_of[nm] = sl
                w = list(ws_user_done[sl])
                if t == 0:
                    w.append(("dG", conv.get(nm, conv.get(nm.replace("T", ""), 0))
                              if nm in conv else conv[nm]))
                if nm == "ws0":
                    ld_tick[nm] = load_w3(WS[sl], sc["WsT"], 0, waits=w)
                elif nm == "ws1":
                    ld_tick[nm] = load_w3(WS[sl], sc["WsT"], 1, waits=w)
                elif nm.startswith("e"):
                    ld_tick[nm] = load_w3(WS[sl], sc["WeT"], int(nm[1:]), waits=w)
                else:
                    ld_tick[nm] = load_w(WS[sl], sc[nm], 0, waits=w)

            issue_load()
            issue_load()

            def group_done(nm):
                ws_user_done[slot_of[nm]] = prog.now("tS")
                if next_load[0] < len(names):
                    issue_load()

            rp = prog.emit("sync", lambda sy, t=t:
                           sy.dma_start(rope_sb[:, :, :], sc["rope"][t, :, :]),
                           waits=list(rope_used) + [("dG", conv[f"rope{t}"])],
                           inc="dW", by=16)

            prog.serial = serial and "l1ln" not in loose
            gw = prog.now("gS")
            if t == 0:
                prog.emit("vector", lambda v: v.memset(xn_sb[:, :, 0:1], 0.0),
                          waits=list(mg_prev), inc="vS")
            vxn = emit_ln(t, p01_w, p23_w, xn_w=gw)
            for c in range(C):
                prog.emit("vector", lambda v, c=c:
                          v.tensor_scalar(arc(XH, c), xnc(c), 0.5, None, ALU.mult),
                          inc=("vS" if c == C - 1 else None))
            vxh = prog.at("vS")
            prog.serial = serial and "gat" not in loose
            for i in range(C):
                prog.emit("tensor", lambda e, i=i:
                          e.matmul(P[4][0:16, :],
                                   CB("CENT", cols=slice(16 * i, 16 * i + 16)),
                                   xnc(i), start=(i == 0), stop=(i == C - 1)),
                          waits=([("vS", vxn)] if i == 0 else []),
                          inc=("tS" if i == C - 1 else None))
            tg = prog.at("tS")
            prog.emit("scalar", lambda s:
                      s.activation(misc32[0:16, :], P[4][0:16, :], AF.Copy),
                      waits=[("tS", tg)], inc="sS")
            sg = prog.now("sS")
            if gdump is not None:
                prog.emit("vector", lambda v: v.tensor_copy(gdump[0:16, 32:48],
                                                            misc32[0:16, 0:16]),
                          waits=sg, inc="vS")
            vtr = []
            for nb in range(4):
                ttr = prog.emit("tensor", lambda e, nb=nb:
                                e.transpose(P[5][:, 0:16],
                                            misc32[0:16, 128 * nb:128 * (nb + 1)],
                                            C3("ID32", rows=slice(0, 16),
                                               cols=slice(0, 16))),
                                waits=sg + vtr, inc="tS")
                vv = prog.emit("vector", lambda v, nb=nb:
                               v.tensor_copy(gt32[:, 16 * nb:16 * (nb + 1)],
                                             P[5][:, 0:16]),
                               waits=[("tS", ttr)], inc="vS")
                vtr = [("vS", vv)]
            if gdump is not None:
                prog.emit("vector", lambda v: v.tensor_copy(gdump[:, 0:16],
                                                            gt32[:, 0:16]),
                          waits=vtr, inc="vS")
            prog.emit("vector", lambda v: v.nop(),
                      waits=prog.now("tS", "sS", "gS", "dW", "dR", "dG", "dO"),
                      inc="vS")
            nm1 = misc32[:, 0:1]
            sm_ = misc32[:, 2:3]
            rs_ = misc32[:, 8:9]
            m2_ = misc32[:, 4:5]

            def rep2_in(coloff, w):
                return bass.AP(gt32, coloff, [[80, 128], [0, 2], [1, w]])

            def out2(c0):
                return misc32[:, c0:c0 + 2].rearrange("p (a b) -> p a b", b=1)

            tmp16 = gt32[:, 64:80]
            for nb in range(4):
                blk = gt32[:, 16 * nb:16 * nb + 16]
                prog.emit("vector", lambda v, blk=blk:
                          v.tensor_add(blk, blk, C3("EBB")),
                          waits=(vtr if nb == 0 else []))
                prog.emit("vector", lambda v, nb=nb:
                          v.reduce_max(out2(0), rep2_in(16 * nb, 16), AX.X))
                prog.emit("vector", lambda v:
                          v.tensor_scalar(misc32[:, 0:4], misc32[:, 0:4], -1.0,
                                          None, ALU.mult), inc="vS")
                vs0 = prog.now("vS")
                prog.emit("scalar", lambda s, blk=blk:
                          s.activation(blk, blk, AF.Exp, bias=nm1), waits=vs0, inc="sS")
                ss0 = prog.now("sS")
                prog.emit("vector", lambda v, nb=nb:
                          v.reduce_sum(out2(2), rep2_in(16 * nb, 16), AX.X), waits=ss0)
                prog.emit("vector", lambda v:
                          v.reciprocal(misc32[:, 8:12], misc32[:, 2:6]))
                prog.emit("vector", lambda v, blk=blk:
                          v.tensor_scalar(blk, blk, rs_, None, ALU.mult))
                prog.emit("vector", lambda v, nb=nb:
                          v.reduce_max(out2(4), rep2_in(16 * nb, 16), AX.X))
                prog.emit("vector", lambda v, blk=blk:
                          v.tensor_scalar(tmp16, blk, m2_, None, ALU.subtract))
                prog.emit("vector", lambda v:
                          v.tensor_scalar(tmp16, tmp16, 0.0, None, ALU.is_ge))
                prog.emit("vector", lambda v, blk=blk:
                          v.scalar_tensor_tensor(tmp16, tmp16, -1e30, blk,
                                                 ALU.mult, ALU.add))
                prog.emit("vector", lambda v, nb=nb:
                          v.reduce_max(out2(4), bass.AP(gt32, 64, [[80, 128], [0, 2], [1, 16]]),
                                       AX.X))
                prog.emit("vector", lambda v, blk=blk:
                          v.tensor_scalar(tmp16, blk, m2_, None, ALU.is_ge))
                prog.emit("vector", lambda v, blk=blk:
                          v.tensor_mul(blk, blk, tmp16), inc="vS")
            vg = prog.at("vS")
            str_ = []
            for nb in range(4):
                ttr = prog.emit("tensor", lambda e, nb=nb:
                                e.transpose(P[5][0:16, 0:128],
                                            gt32[:, 16 * nb:16 * nb + 16], C3("ID32")),
                                waits=[("vS", vg)] + str_, inc="tS")
                se = prog.emit("scalar", lambda s, nb=nb:
                               s.activation(g_sb[0:16, 128 * nb:128 * (nb + 1)],
                                            P[5][0:16, 0:128], AF.Copy),
                               waits=[("tS", ttr)], inc="sS")
                str_ = [("sS", se)]
            gdma = prog.emit("scalar", lambda s: s.dma_start(s_g[:, :], g_sb[:, :]),
                             waits=str_ + list(sg_busy), inc="dR", by=16)

            def qk_rope(dstbuf, WTnm, Bname, vfree0):
                ev_pair = [[("vS", vfree0)], [("vS", vfree0)]]
                last = 0
                for cp in range(4):
                    plo, phi = P[2 * (cp % 2)], P[1 + 2 * (cp % 2)]
                    w0 = [("dW", ld_tick[WTnm])] + ev_pair[cp % 2]
                    for (pp, cc) in ((plo, cp), (phi, cp + 4)):
                        for i in range(C):
                            prog.emit("tensor", lambda e, cc=cc, i=i, pp=pp, WTnm=WTnm:
                                      e.matmul(pp[:, :],
                                               WS[slot_of[WTnm]][:, i,
                                                                 128 * cc:128 * (cc + 1)],
                                               xnc(i), start=(i == 0),
                                               stop=(i == C - 1)),
                                      waits=(w0 if (i == 0 and cc == cp) else []),
                                      inc=("tS" if i == C - 1 else None))
                    tqk = prog.at("tS")
                    cosv = rope_sb[:, cp, :]
                    sinv = rope_sb[:, 4 + cp, :]
                    blo = C3(Bname, cols=slice(cp, cp + 1))
                    bhi = C3(Bname, cols=slice(cp + 4, cp + 5))
                    wv = [("tS", tqk), ("dW", rp)]
                    prog.emit("vector", lambda v, plo=plo, blo=blo, cosv=cosv:
                              v.scalar_tensor_tensor(xt32[0][:, :], plo[:, :], blo, cosv,
                                                     ALU.add, ALU.mult), waits=wv)
                    prog.emit("vector", lambda v, phi=phi, bhi=bhi, sinv=sinv:
                              v.scalar_tensor_tensor(xt32[1][:, :], phi[:, :], bhi, sinv,
                                                     ALU.add, ALU.mult))
                    prog.emit("vector", lambda v, dstbuf=dstbuf, cp=cp:
                              v.tensor_sub(_ch(dstbuf, cp), xt32[0][:, :], xt32[1][:, :]))
                    prog.emit("vector", lambda v, plo=plo, blo=blo, sinv=sinv:
                              v.scalar_tensor_tensor(xt32[0][:, :], plo[:, :], blo, sinv,
                                                     ALU.add, ALU.mult))
                    prog.emit("vector", lambda v, phi=phi, bhi=bhi, cosv=cosv:
                              v.scalar_tensor_tensor(xt32[1][:, :], phi[:, :], bhi, cosv,
                                                     ALU.add, ALU.mult))
                    last = prog.emit("vector", lambda v, dstbuf=dstbuf, cp=cp:
                                     v.tensor_add(_ch(dstbuf, cp + 4), xt32[0][:, :], xt32[1][:, :]),
                                     inc="vS")
                    ev_pair[cp % 2] = [("vS", last)]
                return last

            prog.serial = serial and "qkv" not in loose
            vq = qk_rope(QROT, "Wq1T", "BQ", vxn)
            group_done("Wq1T")
            vk = qk_rope(k_bufs[cur], "Wk1T", "BK", vq)
            group_done("Wk1T")
            vev = []
            sv1 = 0
            for nb in range(4):
                w0 = ([("dW", ld_tick["Wv1T"])] + str_ + [("vS", vxn)]
                      if nb == 0 else []) + vev
                for i in range(C):
                    for hf in range(2):
                        prog.emit("tensor", lambda e, nb=nb, i=i, hf=hf:
                                  e.matmul(P[5 + hf][:, :],
                                           xn_sb[:, i, 1 + 128 * nb:1 + 128 * (nb + 1)],
                                           WS[slot_of["Wv1T"]][:, i,
                                                               512 * hf:512 * (hf + 1)],
                                           start=(i == 0), stop=(i == C - 1)),
                                  waits=(w0 if (i == 0 and hf == 0) else []),
                                  inc=("tS" if (i == C - 1 and hf == 1) else None))
                tvv = prog.at("tS")
                vev = []
                for hf in range(2):
                    sv1 = prog.emit("scalar", lambda s, nb=nb, hf=hf, cur=cur:
                                    s.activation(
                                        v_bufs[cur][:, nb, 8 * hf:8 * hf + 8, 0:64],
                                        P[5 + hf][:, :], AF.Copy),
                                    waits=[("tS", tvv)], inc="sS")
                    vev = [("sS", sv1)]
            group_done("Wv1T")
            prog.serial = serial and "mh" not in loose
            tk2 = 0
            for c in range(C):
                si, w_sq = r_sq.acquire()
                prog.emit("scalar", lambda s, c=c, si=si, cur=cur:
                          s.activation(SQ[si], k_bufs[cur][:, c, :], AF.Square),
                          waits=w_sq + ([("vS", vk)] if c == 0 else []), inc="sS")
                s_now = prog.now("sS")
                tk2 = prog.emit("tensor", lambda e, c=c, si=si:
                                e.matmul(P[4][0:16, :],
                                         CB("HSEL", cols=slice(16 * c, 16 * c + 16)),
                                         SQ[si], start=(c == 0), stop=(c == C - 1)),
                                waits=s_now, inc="tS")
                r_sq.release(si, "tS", tk2)
            prog.emit("vector", lambda v, cur=cur:
                      v.reduce_max(mk2[0:16, 4 * cur:4 * cur + 4]
                                   .rearrange("p (a b) -> p a b", b=1),
                                   bass.AP(P[4], 0, [[512, 16], [0, 4], [1, 512]]),
                                   AX.X),
                      waits=[("tS", tk2)])
            vmk = prog.emit("vector", lambda v: v.nop(), inc="vS")
            tq2 = 0
            for c in range(C):
                si, w_sq = r_sq.acquire()
                prog.emit("scalar", lambda s, c=c, si=si:
                          s.activation(SQ[si], _ch(QROT, c), AF.Square),
                          waits=w_sq + ([("vS", vq)] if c == 0 else []), inc="sS")
                s_now = prog.now("sS")
                tq2 = prog.emit("tensor", lambda e, c=c, si=si:
                                e.matmul(P[4][0:16, :],
                                         CB("HSEL", cols=slice(16 * c, 16 * c + 16)),
                                         SQ[si], start=(c == 0), stop=(c == C - 1)),
                                waits=s_now + ([("vS", vmk)] if c == 0 else []),
                                inc="tS")
                r_sq.release(si, "tS", tq2)
            prog.emit("vector", lambda v:
                      v.tensor_scalar(misc32[0:16, :], P[4][0:16, :], mk2[0:16, 0:1],
                                      0.5, ALU.add, ALU.mult),
                      waits=[("tS", tq2)])
            prog.emit("vector", lambda v:
                      v.tensor_scalar(stat32[0:16, 0:512], P[4][0:16, :], mk2[0:16, 4:5],
                                      0.5, ALU.add, ALU.mult))
            vmh = prog.emit("vector", lambda v:
                            v.tensor_max(mhat[:, :], misc32[0:16, :], stat32[0:16, 0:512]),
                            waits=[], inc="vS")

            prog.serial = serial and "attn" not in loose
            js = list(range(8)) if t > 0 else list(range(4, 8))
            sc_free = [[("vS", vmh)], [("vS", vmh)]]
            pv_banks, bc_banks = [2, 5], [3, 6]
            pv_free = [[("vS", vmh)], [("sS", sv1)]]
            bc_free = [[("vS", vmh)], [("sS", sv1)]]
            attn_v = 0
            for h in range(H):
                pvb, bcb = P[pv_banks[h % 2]], P[bc_banks[h % 2]]
                r0 = 64 * (h % 2)
                ch = h // 2
                for jn, j in enumerate(js):
                    jj = j % 4
                    kb = k_bufs[prv] if j < 4 else k_bufs[cur]
                    vb = v_bufs[prv] if j < 4 else v_bufs[cur]
                    w_, q0, dl = W_J[j], Q0_J[j], DLT_J[j]
                    psb = P[jn % 2]
                    prog.emit("tensor", lambda e, kb=kb, jj=jj, r0=r0, ch=ch,
                              psb=psb, q0=q0, w_=w_:
                              e.matmul(psb[0:128, 0:w_],
                                       kb[r0:r0 + 64, ch, 128 * jj:128 * (jj + 1)],
                                       QROT[r0:r0 + 64, T * ch + q0:T * ch + q0 + w_],
                                       start=True, stop=False),
                              waits=sc_free[jn % 2])
                    tsc = prog.emit("tensor", lambda e, h=h, psb=psb, q0=q0, w_=w_:
                                    e.matmul(psb[0:128, 0:w_],
                                             CB("SELH", rows=slice(0, 16),
                                                cols=slice(128 * h, 128 * h + 128)),
                                             mhat[0:16, q0:q0 + w_],
                                             start=False, stop=True), inc="tS")
                    si, w_pr = r_prb.acquire()
                    sexp = prog.emit("scalar", lambda s, si=si, psb=psb, w_=w_:
                                     s.activation(PRB[si][:, 0:w_], psb[0:128, 0:w_],
                                                  AF.Exp, scale=0.125),
                                     waits=[("tS", tsc)] + w_pr, inc="sS")
                    sc_free[jn % 2] = [("sS", sexp)]
                    eng = "vector" if (j % 2 == 0) else "gpsimd"
                    msem = "vS" if eng == "vector" else "gS"
                    mm_ = prog.emit(eng, lambda v, si=si, w_=w_, dl=dl:
                                    v.tensor_mul(PRB[si][:, 0:w_], PRB[si][:, 0:w_],
                                                 CB("BIGM", cols=slice(dl, dl + w_))),
                                    waits=[("sS", sexp)], inc=msem)
                    wpv = (pv_free[h % 2] if j == js[0] else [])
                    tpv = prog.emit("tensor",
                                    lambda e, vb=vb, jj=jj, h=h, si=si, q0=q0, w_=w_,
                                    pvb=pvb, first=(j == js[0]):
                                    e.matmul(pvb[0:65, q0:q0 + w_], vb[:, jj, h, :],
                                             PRB[si][:, 0:w_], start=first, stop=False),
                                    waits=[(msem, mm_)] + wpv, inc="tS")
                    r_prb.release(si, "tS", tpv)
                tpa = prog.at("tS")
                sat = prog.emit("scalar", lambda s, pvb=pvb, r0=r0, ch=ch:
                                s.activation(ATT1[r0:r0 + 64, T * ch:T * (ch + 1)],
                                             pvb[0:64, :], AF.Copy),
                                waits=[("tS", tpa)], inc="sS")
                prog.emit("vector", lambda v, pvb=pvb:
                          v.tensor_scalar(misc32[64:65, :], pvb[64:65, :], 1.0, 1e-6,
                                          ALU.mult, ALU.add), waits=[("tS", tpa)])
                vrc = prog.emit("vector", lambda v:
                                v.reciprocal(misc32[64:65, :], misc32[64:65, :]),
                                inc="vS")
                tbc = prog.emit("tensor", lambda e, bcb=bcb:
                                e.matmul(bcb[0:64, :],
                                         C3("ONE32", rows=slice(64, 65),
                                            cols=slice(0, 64)),
                                         misc32[64:65, :], start=True, stop=True),
                                waits=[("vS", vrc)] + bc_free[h % 2], inc="tS")
                amv = prog.emit("vector", lambda v, bcb=bcb, r0=r0, ch=ch:
                                v.tensor_mul(ATT1[r0:r0 + 64, T * ch:T * (ch + 1)],
                                             ATT1[r0:r0 + 64, T * ch:T * (ch + 1)],
                                             bcb[0:64, :]),
                                waits=[("tS", tbc), ("sS", sat)], inc="vS")
                pv_free[h % 2] = [("vS", amv)]
                bc_free[h % 2] = [("vS", amv)]
                attn_v = amv

            prog.serial = serial and "mega" not in loose
            if t == 0:
                for o in range(C):
                    for i in range(C):
                        prog.emit("tensor", lambda e, o=o, i=i:
                                  e.matmul(P[7][:, o:o + 1],
                                           WS[slot_of["WoT"]][:, i,
                                                              128 * o:128 * (o + 1)],
                                           CB("BVBF", cols=slice(i, i + 1)),
                                           start=(o == 0 and i == 0), stop=False),
                                  waits=([("dW", ld_tick["WoT"])]
                                         if (o == 0 and i == 0) else []),
                                  inc=("tS" if (o == C - 1 and i == C - 1) else None))
                prog.emit("vector", lambda v:
                          v.tensor_add(C3("BALL"), C3("BALL"), P[7][:, 0:8]),
                          waits=prog.now("tS"), inc="vS")
            ball_v = prog.at("vS")

            wm = [("vS", attn_v), ("vS", ball_v), ("dW", ld_tick["WoT"])] + \
                list(mg_prev) + prog.now("sS")
            for o in range(C):
                for i in range(C):
                    prog.emit("tensor", lambda e, o=o, i=i:
                              e.matmul(P[o][:, :],
                                       WS[slot_of["WoT"]][:, i, 128 * o:128 * (o + 1)],
                                       _ch(ATT1, i), start=(i == 0), stop=False),
                              waits=(wm if (o == 0 and i == 0) else []),
                              inc=("tS" if i == C - 1 else None))
            group_done("WoT")
            for o in range(C):
                prog.emit("tensor", lambda e, o=o:
                          e.matmul(P[o][:, :], IDB, xnc(o), start=False, stop=False))
                prog.emit("tensor", lambda e, o=o:
                          e.matmul(P[o][:, :],
                                   CB("BR", rows=slice(0, 16),
                                      cols=slice(128 * o, 128 * o + 128)),
                                   g_sb[0:16, :], start=False, stop=False),
                          inc="tS")
            for nm2 in ("ws0", "ws1"):
                for o in range(C):
                    for i in range(C):
                        prog.emit("tensor", lambda e, o=o, i=i, nm2=nm2:
                                  e.matmul(P[o][:, :],
                                           WS[slot_of[nm2]][:, i, 128 * o:128 * (o + 1)],
                                           arc(XH, i), start=False, stop=False),
                                  waits=([("dW", ld_tick[nm2]), ("vS", vxh)]
                                         if (o == 0 and i == 0) else []),
                                  inc=("tS" if i == C - 1 else None))
                group_done(nm2)
            gb_dma = {}
            for e_ in range(2):
                gb_dma[e_] = prog.emit(
                    "scalar", lambda s, e_=e_:
                    s.dma_start(GB[e_ % 2], bass.AP(s_g, e_ * T, [[0, 128], [1, T]])),
                    waits=[("dR", gdma)] + gb_free[e_ % 2], inc="dR", by=16)
            for e_ in range(NE):
                for i in range(C):
                    eng = "vector" if i % 2 == 0 else "gpsimd"
                    msem = "vS" if eng == "vector" else "gS"
                    si, w_zu = r_zu.acquire()
                    zv = prog.emit(eng, lambda v, i=i, si=si, e_=e_:
                                   v.tensor_mul(ZU[si], xnc(i), GB[e_ % 2]),
                                   waits=[("dR", gb_dma[e_])] + w_zu, inc=msem)
                    tz = 0
                    for o in range(C):
                        tz = prog.emit(
                            "tensor", lambda e, o=o, i=i, si=si, e_=e_:
                            e.matmul(P[o][:, :],
                                     WS[slot_of[f"e{e_}"]][:, i, 128 * o:128 * (o + 1)],
                                     ZU[si], start=False, stop=False),
                            waits=([("dW", ld_tick[f"e{e_}"]), (msem, zv)]
                                   if o == 0 else []),
                            inc=("tS" if o == C - 1 else None))
                    r_zu.release(si, "tS", tz)
                gb_free[e_ % 2] = prog.now("tS")
                group_done(f"e{e_}")
                if e_ + 2 < NE:
                    gb_dma[e_ + 2] = prog.emit(
                        "scalar", lambda s, e_=e_:
                        s.dma_start(GB[e_ % 2],
                                    bass.AP(s_g, (e_ + 2) * T, [[0, 128], [1, T]])),
                        waits=gb_free[e_ % 2], inc="dR", by=16)
            sg_busy = prog.now("dR")
            tmega = prog.at("tS")
            prog.serial = serial and "evac" not in loose
            for o in range(C):
                si, w_o = r_ost.acquire()
                prog.emit("vector", lambda v, o=o, si=si, t=t:
                          v.scalar_tensor_tensor(ost[si][:, :], P[o][:, :],
                                                 C3("BALL", cols=slice(o, o + 1)),
                                                 xc(o, t), ALU.add, ALU.add),
                          waits=[("tS", tmega)] + w_o, inc="vS")
                vos = prog.now("vS")
                dt_ = prog.emit("sync", lambda sy, o=o, si=si, t=t:
                                sy.dma_start(
                                    bass.AP(out_d, o * 128 * S + T * t,
                                            [[S, 128], [1, T]]), ost[si][:, :]),
                                waits=vos, inc="dO", by=16)
                r_ost.release(si, "dO", dt_)
            mg_prev = prog.now("vS")
            rope_used = [("vS", vk)]
            p01_w = prog.now("vS")
            p23_w = prog.now("vS")
            ws_free = [prog.now("tS"), prog.now("tS")]
            if stop_after == "l1t0":
                break

        _finish(nc, prog, dbg, dbg_d, locals())
    return nc, din, out_d, dbg_d


def _ch(buf, c):
    if len(buf.shape) == 3:
        return buf[:, c, :]
    return buf[:, T * c:T * (c + 1)]


def _finish(nc, prog, dbg, dbg_d, env):
    for nm in dbg:
        ten = env.get(nm)
        if ten is None:
            continue
        shp = list(ten.shape)
        flat = [shp[0], int(np.prod(shp[1:]))]
        dd = nc.dram_tensor("dbg_" + nm, flat, F32, kind="ExternalOutput")
        dbg_d[nm] = dd
        src = ten[tuple(slice(None) for _ in shp)]
        if len(shp) > 2:
            src = src.rearrange("p " + " ".join(f"a{i}" for i in range(len(shp) - 1))
                                + " -> p (" + " ".join(f"a{i}" for i in range(len(shp) - 1))
                                + ")")
        prog.emit("gpsimd", lambda g, dd=dd, src=src: g.dma_start(dd[:, :], src),
                  waits=prog.now("vS", "sS", "tS", "gS"), inc="dG", by=16)
    prog.emit("sync", lambda sy: sy.nop(),
              waits=prog.now("dO", "dG", "dW", "dR"))
    prog.replay_block()



def _host_consts(inputs):
    f32 = np.float32
    cbf = np.zeros((128, CBF_W), f32)
    c32 = np.zeros((128, C32_W), f32)

    def put_bf(name, arr):
        a, _ = CBF_COLS[name]
        cbf[:arr.shape[0], a:a + arr.shape[1]] = arr

    def put_32(name, arr):
        a, _ = C32_COLS[name]
        c32[:arr.shape[0], a:a + arr.shape[1]] = arr

    hneg = np.zeros((16, 1024), f32)
    for h in range(16):
        hneg[h, 64 * h:64 * (h + 1)] = -1.0
    put_bf("HNEG", hneg)
    i_ = np.arange(128)[:, None]
    p_ = np.arange(640)[None, :]
    put_bf("BIGM", ((p_ >= i_) & (p_ <= i_ + 512)).astype(f32))
    rf = np.asarray(inputs["L0_attn_random_features"], f32)
    rb = np.zeros((128, 128), f32)
    rb[0:64, 0:64] = rf.T
    rb[64:128, 64:128] = rf.T
    put_bf("RFBD", rb)
    put_bf("IDB", np.eye(128, dtype=f32))
    put_bf("ONE", np.ones((128, 128), f32))
    put_bf("BR", np.asarray(inputs["L1_ffn_br"], f32))
    centT = np.asarray(inputs["L1_ffn_centroids"], f32).T
    put_bf("CENT", centT.reshape(C, 128, 16).transpose(1, 0, 2).reshape(128, C * 16))
    hsel = np.zeros((128, 128), f32)
    for c in range(C):
        for p in range(128):
            hsel[p, 16 * c + 2 * c + p // 64] = 1.0
    put_bf("HSEL", hsel)
    put_bf("BVBF", np.asarray(inputs["L1_attn_bv"], f32).reshape(C, 128).T)
    selh = np.zeros((16, 2048), f32)
    for h in range(16):
        selh[h, 128 * h:128 * (h + 1)] = -1.0
    put_bf("SELH", selh)

    put_32("ID32", np.eye(128, dtype=f32))

    def pp(vec):
        return np.asarray(vec, f32).reshape(C, 128).T

    put_32("MUR", pp(inputs["L0_ffn_mu_r"]))
    put_32("MUK", pp(inputs["L0_ffn_mu_k"]))
    put_32("BOUT", pp(inputs["L0_attn_b_out"]))
    put_32("BQ", pp(inputs["L1_attn_bq"]))
    put_32("BK", pp(inputs["L1_attn_bk"]))
    put_32("BO", pp(inputs["L1_attn_bo"]))
    put_32("BS0", pp(inputs["L1_ffn_bs"][0]))
    put_32("BS1", pp(inputs["L1_ffn_bs"][1]))
    put_32("EB", np.asarray(inputs["L1_ffn_expert_biases"], f32).reshape(16, 1))
    put_32("EBB", np.broadcast_to(
        np.asarray(inputs["L1_ffn_expert_biases"], f32)[None, :], (128, 16)).copy())
    put_32("ONE32", np.ones((128, 128), f32))
    put_32("EPS5", np.full((128, 1), 1e-5, f32))
    put_32("EPS30", np.full((128, 1), 1e-30, f32))
    return cbf, c32


def _host_rope():
    f32 = np.float32
    half = D // 2
    inv = 1.0 / (10000.0 ** (np.arange(half, dtype=f32) / half))
    n = np.arange(S, dtype=f32)
    ang = inv[:, None] * n[None, :]
    cos, sin = np.cos(ang).astype(f32), np.sin(ang).astype(f32)
    out = np.zeros((NT, 128, C * T), f32)
    for t in range(NT):
        for c in range(4):
            out[t, :, T * c:T * (c + 1)] = cos[128 * c:128 * (c + 1), T * t:T * (t + 1)]
            out[t, :, T * (4 + c):T * (5 + c)] = \
                sin[128 * c:128 * (c + 1), T * t:T * (t + 1)]
    return out


def make_inputs(inputs):
    f32 = np.float32
    x = np.asarray(inputs["L0_attn_x"], f32)
    cbf, c32 = _host_consts(inputs)
    rope = _host_rope()

    def tr(a):
        return np.ascontiguousarray(np.asarray(a, f32).T)

    shared = dict(
        wqkvT=tr(inputs["L0_attn_w_qkv"]), woutT=tr(inputs["L0_attn_w_out"]),
        WrT=tr(inputs["L0_ffn_W_r"]), WkT=tr(inputs["L0_ffn_W_k"]),
        WvT=tr(inputs["L0_ffn_W_v"]), Wq1T=tr(inputs["L1_attn_Wq"]),
        Wk1T=tr(inputs["L1_attn_Wk"]), Wv1T=tr(inputs["L1_attn_Wv"]),
        WoT=tr(inputs["L1_attn_Wo"]),
        WsT=np.ascontiguousarray(
            np.asarray(inputs["L1_ffn_Ws"], f32).transpose(0, 2, 1)),
        WeT=np.ascontiguousarray(
            np.asarray(inputs["L1_ffn_Wr"], f32).transpose(0, 2, 1)),
        ropein=rope, cbf=cbf, c32=c32,
    )
    in_maps = []
    for b in range(8):
        m = dict(shared)
        m["x_fm"] = np.ascontiguousarray(x[b].T)
        in_maps.append(m)
    return in_maps


def kernel(**inputs):
    nc, din, out_d, dbg_d = build_nc(serial=True)
    in_maps = make_inputs(inputs)
    res = run_bass_kernel_spmd(nc, in_maps, core_ids=list(range(8)))
    outs = [np.ascontiguousarray(res.results[b]["out_fm"].T) for b in range(8)]
    return np.stack(outs, axis=0)
